# version 04
# baseline (speedup 1.0000x reference)
import numpy as np
import ml_dtypes
from contextlib import ExitStack

import concourse.bass as bass
import concourse.mybir as mybir
from concourse.bass_utils import run_bass_kernel_spmd

F32 = mybir.dt.float32
BF16 = mybir.dt.bfloat16
ALU = mybir.AluOpType
AF = mybir.ActivationFunctionType
AX = mybir.AxisListType.X

T, D = 2048, 1024
NQT = T // 128
NDC = D // 128
QC = 512
NQC = T // QC

C23 = 12582912.0
C52 = 4503599627370496.0
EPS_DOM = 9.094947017729282e-13
NEG_BIG = -1.0e30
CLAMP_MAX = float(np.log(1.0 / 0.01))
LN1E4_64 = float(np.log(10000.0) / 64.0)
C_LN_HI = float(np.float32(LN1E4_64))
C_LN_LO = float(LN1E4_64 - C_LN_HI)
TWO_PI = 2.0 * float(np.pi)
C2PI_HI = float(np.float32(np.trunc(TWO_PI * 2048.0) / 2048.0))
C2PI_MID = float(np.float32(TWO_PI - C2PI_HI))
INV_2PI = float(np.float32(1.0 / TWO_PI))
HALF_PI = float(np.float32(np.pi / 2.0))


class Buf:
    __slots__ = ("ap", "wtok", "rtoks", "name")

    def __init__(self, ap, name=""):
        self.ap = ap
        self.name = name
        self.wtok = None
        self.rtoks = []


class Prog:
    ENGS = ("sync", "vector", "scalar", "tensor", "gpsimd")

    def __init__(self, nc, es):
        self.nc = nc
        self.ops = {e: [] for e in self.ENGS}
        self.cnt = {}
        self.sems = {}
        for s in ("sd", "gd", "ad", "v", "a", "t", "g", "sf"):
            self.sems[s] = es.enter_context(nc.semaphore(s))
            self.cnt[s] = 0

    def _sem_for(self, eng, dma):
        if eng == "sync":
            return ("sd", 16) if dma else ("sf", 1)
        if dma and eng == "gpsimd":
            return "gd", 16
        if dma and eng == "scalar":
            return "ad", 16
        return {"vector": "v", "scalar": "a", "tensor": "t", "gpsimd": "g"}[eng], 1

    def do(self, eng, fn, reads=(), writes=(), extra=(), dma=False):
        sem, amt = self._sem_for(eng, dma)
        waits = {}

        def add(tok):
            if tok is not None:
                s, v = tok
                waits[s] = max(waits.get(s, 0), v)
        for b in reads:
            add(b.wtok)
        for b in writes:
            add(b.wtok)
            for tok in b.rtoks:
                add(tok)
        for tok in extra:
            add(tok)
        self.cnt[sem] += amt
        tok = (sem, self.cnt[sem])
        self.ops[eng].append((waits, fn, sem, amt))
        for b in reads:
            b.rtoks.append(tok)
        for b in writes:
            b.wtok = tok
            b.rtoks = []
        return tok

    def dma_f(self, eng, fn, reads=(), writes=(), extra=()):
        ex = [t for t in extra if t is not None]
        for b in reads:
            if b.wtok is not None:
                ex.append(b.wtok)
        for b in writes:
            if b.wtok is not None:
                ex.append(b.wtok)
            ex.extend(b.rtoks)
        t = self.do(eng, fn, extra=ex, dma=True)
        tg = self.do(eng, lambda e: e.nop(), extra=[t])
        for b in reads:
            b.rtoks.append(tg)
        for b in writes:
            b.wtok = tg
            b.rtoks = []
        return tg

    def dma_s(self, fn, reads=(), writes=(), extra=()):
        return self.dma_f("sync", fn, reads, writes, extra)

    def dma_a(self, fn, reads=(), writes=(), extra=()):
        return self.dma_f("scalar", fn, reads, writes, extra)

    def dma_g(self, fn, reads=(), writes=(), extra=()):
        ex = [t for t in extra if t is not None]
        for b in reads:
            if b.wtok is not None:
                ex.append(b.wtok)
        for b in writes:
            if b.wtok is not None:
                ex.append(b.wtok)
            ex.extend(b.rtoks)
        t = self.do("gpsimd", fn, extra=ex, dma=True)
        tg = self.do("gpsimd", lambda e: e.nop(), extra=[t])
        for b in reads:
            b.rtoks.append(tg)
        for b in writes:
            b.wtok = tg
            b.rtoks = []
        return tg

    def flush_g(self, bufs):
        ex = []
        for b in bufs:
            if b.wtok is not None:
                ex.append(b.wtok)
            ex.extend(b.rtoks)
        tg = self.do("gpsimd", lambda e: e.nop(), extra=ex)
        for b in bufs:
            b.wtok = tg
            b.rtoks = []
        return tg

    def barrier(self):
        toks = [(s, self.cnt[s]) for s in self.cnt]
        for eng in self.ENGS:
            self.do(eng, lambda e: e.nop(), extra=toks)

    def emit(self, block):
        waited = {e: {} for e in self.ENGS}

        def runner(eng_name):
            def _(eng):
                w = waited[eng_name]
                for waits, fn, sem, amt in self.ops[eng_name]:
                    for s in sorted(waits):
                        v = waits[s]
                        if w.get(s, 0) < v:
                            eng.wait_ge(self.sems[s], v)
                            w[s] = v
                    inst = fn(eng)
                    inst.then_inc(self.sems[sem], amt)
            return _
        block.sync(runner("sync"))
        block.vector(runner("vector"))
        block.scalar(runner("scalar"))
        block.tensor(runner("tensor"))
        block.gpsimd(runner("gpsimd"))


def build(debug_stop=None):
    nc = bass.Bass("TRN2", target_bir_lowering=False)
    es = ExitStack()

    dx = nc.dram_tensor("x", [T, D], F32, kind="ExternalInput").ap()
    dy = nc.dram_tensor("y", [T, D], F32, kind="ExternalOutput").ap()
    d_ident = nc.dram_tensor("ident", [128, 128], BF16, kind="ExternalInput").ap()
    d_cmask = nc.dram_tensor("cmask", [128, 128], F32, kind="ExternalInput").ap()
    d_pos = nc.dram_tensor("pos", [T, 1], F32, kind="ExternalInput").ap()
    d_jrow = nc.dram_tensor("jrow", [1, 64], F32, kind="ExternalInput").ap()
    d_wqkv0 = nc.dram_tensor("l0_wqkv", [8, 8, 128, 384], F32, kind="ExternalInput").ap()
    d_bqkv0 = nc.dram_tensor("l0_bqkv", [8, 128, 3], F32, kind="ExternalInput").ap()
    d_wo0 = nc.dram_tensor("l0_wo", [8, 128, 1024], F32, kind="ExternalInput").ap()
    d_bo0 = nc.dram_tensor("l0_bo", [1, 1024], F32, kind="ExternalInput").ap()
    d_w1t = nc.dram_tensor("l0_w1t", [8, 128, 4096], F32, kind="ExternalInput").ap()
    d_b1 = nc.dram_tensor("l0_b1", [128, 32], F32, kind="ExternalInput").ap()
    d_w2t = nc.dram_tensor("l0_w2t", [32, 128, 1024], F32, kind="ExternalInput").ap()
    d_b2 = nc.dram_tensor("l0_b2", [1, 1024], F32, kind="ExternalInput").ap()
    d_wqkv1 = nc.dram_tensor("l1_wqkv", [8, 8, 128, 384], F32, kind="ExternalInput").ap()
    d_wo1 = nc.dram_tensor("l1_wo", [8, 128, 1024], F32, kind="ExternalInput").ap()
    d_wt1 = nc.dram_tensor("l1_wt", [8, 128, 1024], F32, kind="ExternalInput").ap()
    d_wpt = nc.dram_tensor("l1_wpt", [8, 128, 256], F32, kind="ExternalInput").ap()
    d_bp = nc.dram_tensor("l1_bp", [128, 2], F32, kind="ExternalInput").ap()
    d_simt = nc.dram_tensor("l1_simt", [4, 256], F32, kind="ExternalInput").ap()
    d_temp = nc.dram_tensor("l1_temp", [1, 1], F32, kind="ExternalInput").ap()
    d_wdt = nc.dram_tensor("l1_wdt", [8, 128, 15], F32, kind="ExternalInput").ap()
    d_wut = nc.dram_tensor("l1_wut", [15, 1024], F32, kind="ExternalInput").ap()

    d_hhi = nc.dram_tensor("s_hhi", [4096, T], BF16).ap()
    d_hlo = nc.dram_tensor("s_hlo", [4096, T], BF16).ap()
    d_w1q = nc.dram_tensor("s_w1q", [8, 128, 4096], BF16).ap()
    d_w2q = nc.dram_tensor("s_w2q", [32, 128, 1024], BF16).ap()
    d_hq = nc.dram_tensor("s_hq", [32, 128, T], BF16).ap()
    d_sc = nc.dram_tensor("s_sc", [128, 8], F32).ap()
    d_sc2 = nc.dram_tensor("s_sc2", [1, 8], F32).ap()
    d_ccin = nc.dram_tensor("s_ccin", [1, 8], F32).ap()
    d_ccout = nc.dram_tensor("s_ccout", [1, 8], F32).ap()

    def sb(name, shape, dt=F32):
        return Buf(es.enter_context(nc.sbuf_tensor(name, shape, dt)), name)

    X = [sb(f"X{i}", [128, D]) for i in range(NQT)]
    ident = sb("identt", [128, 128], BF16)
    cmask = sb("cmaskt", [128, 128])
    ident32 = sb("ident32", [128, 128])
    onescol16 = sb("ones16c", [128, 1], BF16)
    ones32 = sb("ones32", [128, 16])
    scal = sb("scal", [128, 16])
    stat = sb("stat", [128, 8])
    const32 = sb("const32", [128, 4])
    gates = sb("gates", [128, 64])
    qsc = sb("qsc", [128, 12])
    XTH = [sb(f"XTH{c}", [128, T], BF16) for c in range(NDC)]
    XTL = [sb(f"XTL{c}", [128, T], BF16) for c in range(NDC)]
    WS32 = sb("WS32", [128, 2048])
    AR16 = sb("AR16", [128, 24704], BF16)
    AR32 = sb("AR32", [128, 4608])

    delta = Buf(scal.ap[:, 12:13], "delta")
    negM = Buf(scal.ap[:, 13:14], "negM")
    bcol = Buf(scal.ap[:, 8:12], "bcol")

    class _Carve:
        def __init__(self, arena, total):
            self.arena = arena
            self.off = 0
            self.total = total

        def take(self, n, rows=128, name=""):
            a = self.arena.ap[0:rows, self.off:self.off + n]
            self.off += n
            assert self.off <= self.total, (name, self.off)
            return Buf(a, name)

    c16 = _Carve(AR16, 24704)
    QH = c16.take(T, name="QH"); QL = c16.take(T, name="QL")
    KH = c16.take(T, name="KH"); KL = c16.take(T, name="KL")
    _vh = c16.take(NQT * 128); _vl = c16.take(NQT * 128)
    VH = [Buf(_vh.ap[:, k * 128:(k + 1) * 128], f"VH{k}") for k in range(NQT)]
    VL = [Buf(_vl.ap[:, k * 128:(k + 1) * 128], f"VL{k}") for k in range(NQT)]
    AH = c16.take(QC, name="AH"); AL = c16.take(QC, name="AL")
    WAH = c16.take(3072, name="WAH"); WAL = c16.take(3072, name="WAL")
    WBH = c16.take(1024, name="WBH"); WBL = c16.take(1024, name="WBL")
    _e16 = c16.take(4 * QC)
    EH = [Buf(_e16.ap[:, i * QC:(i + 1) * QC], f"EH{i}") for i in range(2)]
    EL = [Buf(_e16.ap[:, (2 + i) * QC:(3 + i) * QC], f"EL{i}") for i in range(2)]
    onesrow16b = c16.take(128, rows=33, name="ones16rb")
    _drh = c16.take(QC, rows=33); _drl = c16.take(QC, rows=33)
    drh_b, drl_b = _drh, _drl

    c32 = _Carve(AR32, 4608)
    SC = [c32.take(QC, name=f"SC{i}") for i in range(3)]
    brow = c32.take(1024, name="brow")
    bc32 = c32.take(QC, name="bc32")
    _drow = c32.take(QC, rows=33)
    drow_b = _drow
    _sinbase = c32.take(1024, name="sinb")
    COS = [Buf(brow.ap[:, (i % 16) * 64:(i % 16) * 64 + 64], f"COS{i}") for i in range(NQT)]
    SIN = [Buf(_sinbase.ap[:, (i % 16) * 64:(i % 16) * 64 + 64], f"SIN{i}") for i in range(NQT)]
    h32 = Buf(AR32.ap[:, 2560:4608], "h32")

    PS = [Buf(es.enter_context(nc.psum_tensor(f"PS{i}", [128, QC], F32)), f"PS{i}")
          for i in range(4)]
    PSD = Buf(es.enter_context(nc.psum_tensor("PSD", [64, QC], F32)), "PSD")
    PT = [Buf(es.enter_context(nc.psum_tensor(f"PTx{i}", [128, 128], BF16)), f"PT{i}")
          for i in range(2)]
    PT2 = Buf(es.enter_context(nc.psum_tensor("PT2", [128, 128], F32)), "PT2")

    p = Prog(nc, es)

    def split_hi_lo(src, src_ap, hi, hi_ap, lo, lo_ap):
        p.do("vector", lambda e, i=src_ap, o=hi_ap: e.tensor_copy(out=o, in_=i),
             reads=[src], writes=[hi])
        p.do("vector", lambda e, i0=hi_ap, i1=src_ap, o=lo_ap: e.scalar_tensor_tensor(
            out=o, in0=i0, scalar=-1.0, in1=i1, op0=ALU.mult, op1=ALU.add),
            reads=[hi, src], writes=[lo])

    def mm3(ps, ps_ap, lh, lh_ap, ll, ll_ap, rh, rh_ap, rl, rl_ap, start, stop, tp=None):
        p.do("tensor", lambda e, o=ps_ap, a=lh_ap, b=rh_ap, s=start, t_=tp: e.matmul(
            o, a, b, start=s, stop=False, tile_position=t_), reads=[lh, rh], writes=[ps])
        p.do("tensor", lambda e, o=ps_ap, a=lh_ap, b=rl_ap, t_=tp: e.matmul(
            o, a, b, start=False, stop=False, tile_position=t_), reads=[lh, rl], writes=[ps])
        p.do("tensor", lambda e, o=ps_ap, a=ll_ap, b=rh_ap, s=stop, t_=tp: e.matmul(
            o, a, b, start=False, stop=s, tile_position=t_), reads=[ll, rh], writes=[ps])

    def bcast_scalar(src, src_ap, dst, dst_ap, slot):
        t1 = p.dma_g(lambda e, i=src_ap, o=d_sc2[0:1, slot:slot + 1]:
                     e.dma_start(out=o, in_=i), reads=[src])
        p.dma_g(lambda e, o=dst_ap,
                i=d_sc2[0:1, slot:slot + 1].partition_broadcast(128):
                e.dma_start(out=o, in_=i), writes=[dst], extra=[t1])

    def build_xt():
        for c in range(NDC):
            for qt in range(NQT):
                p.do("tensor", lambda e, i=X[qt].ap[:, c * 128:(c + 1) * 128], o=PT2.ap[:, :]:
                     e.transpose(o, i, ident32.ap[:, :]), reads=[X[qt], ident32], writes=[PT2])
                p.do("vector", lambda e, i=PT2.ap[:, :], o=XTH[c].ap[:, qt * 128:(qt + 1) * 128]:
                     e.tensor_copy(out=o, in_=i), reads=[PT2], writes=[XTH[c]])
                p.do("vector", lambda e, i0=XTH[c].ap[:, qt * 128:(qt + 1) * 128],
                     i1=PT2.ap[:, :], o=XTL[c].ap[:, qt * 128:(qt + 1) * 128]:
                     e.scalar_tensor_tensor(out=o, in0=i0, scalar=-1.0, in1=i1,
                                            op0=ALU.mult, op1=ALU.add),
                     reads=[XTH[c], PT2], writes=[XTL[c]])

    def layernorm_inplace():
        for qt in range(NQT):
            xb = X[qt]
            p.do("vector", lambda e, i=xb.ap[:, :]: e.tensor_reduce(
                out=stat.ap[:, 0:1], in_=i, axis=AX, op=ALU.add), reads=[xb], writes=[stat])
            p.do("vector", lambda e: e.tensor_scalar(
                out=stat.ap[:, 1:2], in0=stat.ap[:, 0:1], scalar1=1.0 / D, scalar2=None,
                op0=ALU.mult), reads=[stat], writes=[stat])
            p.do("scalar", lambda e, i=xb.ap[:, 0:QC]: e.activation(
                out=SC[0].ap, in_=i, func=AF.Square, accum_out=stat.ap[:, 2:3]),
                reads=[xb], writes=[SC[0], stat])
            p.do("scalar", lambda e, i=xb.ap[:, QC:D]: e.activation(
                out=SC[1].ap, in_=i, func=AF.Square, accum_out=stat.ap[:, 3:4]),
                reads=[xb], writes=[SC[1], stat])
            p.do("vector", lambda e: e.tensor_tensor(
                out=stat.ap[:, 2:3], in0=stat.ap[:, 2:3], in1=stat.ap[:, 3:4], op=ALU.add),
                reads=[stat], writes=[stat])
            p.do("vector", lambda e: e.tensor_tensor(
                out=stat.ap[:, 4:5], in0=stat.ap[:, 1:2], in1=stat.ap[:, 1:2], op=ALU.mult),
                reads=[stat], writes=[stat])
            p.do("vector", lambda e: e.tensor_scalar(
                out=stat.ap[:, 5:6], in0=stat.ap[:, 2:3], scalar1=1.0 / D,
                scalar2=stat.ap[:, 4:5], op0=ALU.mult, op1=ALU.subtract),
                reads=[stat], writes=[stat])
            p.do("scalar", lambda e: e.activation(
                out=stat.ap[:, 6:7], in_=stat.ap[:, 5:6], func=AF.Sqrt, bias=1e-5),
                reads=[stat, const32], writes=[stat])
            p.do("vector", lambda e: e.reciprocal(out=stat.ap[:, 7:8], in_=stat.ap[:, 6:7]),
                 reads=[stat], writes=[stat])
            p.do("vector", lambda e, i=xb.ap[:, :]: e.tensor_scalar(
                out=i, in0=i, scalar1=stat.ap[:, 1:2], scalar2=stat.ap[:, 7:8],
                op0=ALU.subtract, op1=ALU.mult), reads=[xb, stat], writes=[xb])

    def store_out():
        toks = []
        for qt in range(NQT):
            toks.append(p.do("scalar", lambda e, i=X[qt].ap[:, :],
                             o=dy[qt * 128:(qt + 1) * 128, :]:
                             e.dma_start(out=o, in_=i), reads=[X[qt]], dma=True))
        p.do("scalar", lambda e: e.nop(), extra=[toks[-1]])

    for qt in range(NQT):
        p.dma_a(lambda e, o=X[qt].ap[:, :], i=dx[qt * 128:(qt + 1) * 128, :]:
                e.dma_start(out=o, in_=i), writes=[X[qt]])
    p.dma_s(lambda e: e.dma_start(out=ident.ap[:, :], in_=d_ident),
         writes=[ident])
    p.dma_s(lambda e: e.dma_start(out=cmask.ap[:, :], in_=d_cmask),
         writes=[cmask])
    p.do("vector", lambda e: e.tensor_copy(out=ident32.ap[:, :], in_=ident.ap[:, :]),
         reads=[ident], writes=[ident32])
    p.do("vector", lambda e: e.memset(ones32.ap[:, :], 1.0), writes=[ones32])
    p.do("vector", lambda e: e.memset(const32.ap[:, 0:1], 1e-5), writes=[const32])
    p.do("vector", lambda e: e.memset(const32.ap[:, 1:2], 1e-6), writes=[const32])
    p.do("vector", lambda e: e.memset(const32.ap[:, 2:3], 0.125), writes=[const32])
    p.do("vector", lambda e: e.memset(const32.ap[:, 3:4], 0.0078125), writes=[const32])
    nc.const_aps.register(F32, 1e-5, const32.ap[:, 0:1])
    nc.const_aps.register(F32, 1e-6, const32.ap[:, 1:2])
    nc.const_aps.register(F32, 0.125, const32.ap[:, 2:3])
    nc.const_aps.register(F32, 0.0078125, const32.ap[:, 3:4])
    p.do("vector", lambda e: e.memset(onesrow16b.ap[:, :], 1.0), writes=[onesrow16b])
    p.do("vector", lambda e: e.memset(onescol16.ap[:, :], 1.0), writes=[onescol16])
    p.do("vector", lambda e: e.tensor_scalar(
        out=scal.ap[:, 0:1], in0=ones32.ap[:, 0:1], scalar1=EPS_DOM, scalar2=None,
        op0=ALU.add), reads=[ones32], writes=[scal])
    p.do("vector", lambda e: e.tensor_scalar(
        out=delta.ap, in0=scal.ap[:, 0:1], scalar1=1.0, scalar2=None,
        op0=ALU.is_le), reads=[scal], writes=[delta])

    def attn_inner(causal, layer, dbg=False):
        nheads = 2 if layer == 0 else 1
        dh = 64 if layer == 0 else 128
        for qc in range(NQC):
            ktmax = NQT if not causal else min(NQT, (qc + 1) * (QC // 128))
            for kt in range(ktmax):
                qlo = qc * QC
                off = kt * 128 - qlo if (causal and kt * 128 > qlo) else 0
                for h in range(nheads):
                    base = h * 64
                    pss = PS[h]
                    sl = slice(base, base + dh)
                    mm3(pss, pss.ap[:, off:QC],
                        KH, KH.ap[sl, kt * 128:(kt + 1) * 128],
                        KL, KL.ap[sl, kt * 128:(kt + 1) * 128],
                        QH, QH.ap[sl, qlo + off:qlo + QC],
                        QL, QL.ap[sl, qlo + off:qlo + QC],
                        start=True, stop=True)
                    e32 = SC[h]
                    if causal and qlo <= kt * 128 < qlo + QC:
                        p.do("vector", lambda e, i=pss.ap[:, off:off + 128],
                             o=SC[1].ap[:, 0:128]: e.tensor_tensor(
                            out=o, in0=i, in1=cmask.ap[:, :], op=ALU.add),
                            reads=[pss, cmask], writes=[SC[1]])
                        p.do("scalar", lambda e, i=SC[1].ap[:, 0:128],
                             o=e32.ap[:, off:off + 128]: e.activation(
                            out=o, in_=i, func=AF.Exp, scale=0.0078125),
                            reads=[SC[1], const32], writes=[e32])
                        if QC - off - 128 > 0:
                            p.do("scalar", lambda e, i=pss.ap[:, off + 128:QC],
                                 o=e32.ap[:, off + 128:QC]: e.activation(
                                out=o, in_=i, func=AF.Exp, scale=0.0078125),
                                reads=[pss, const32], writes=[e32])
                    else:
                        if layer == 0:
                            p.do("scalar", lambda e, i=pss.ap[:, off:QC], o=e32.ap[:, off:QC]:
                                 e.activation(out=o, in_=i, func=AF.Exp, bias=negM.ap,
                                              scale=0.125),
                                 reads=[pss, negM, const32], writes=[e32])
                        else:
                            p.do("scalar", lambda e, i=pss.ap[:, off:QC], o=e32.ap[:, off:QC]:
                                 e.activation(out=o, in_=i, func=AF.Exp, scale=0.0078125),
                                 reads=[pss, const32], writes=[e32])
                    eh, el = EH[h], EL[h]
                    split_hi_lo(e32, e32.ap[:, off:QC], eh, eh.ap[:, off:QC],
                                el, el.ap[:, off:QC])
                    di = h * 32
                    dtp = None if h == 0 else (0, 32)
                    p.do("tensor", lambda e, o=PSD.ap[di:di + 1, off:QC],
                         a=onescol16.ap[:, 0:1], b=eh.ap[:, off:QC], s=(kt == 0), t_=dtp:
                         e.matmul(o, a, b, start=s, stop=False, tile_position=t_),
                         reads=[onescol16, eh], writes=[PSD])
                    p.do("tensor", lambda e, o=PSD.ap[di:di + 1, off:QC],
                         a=onescol16.ap[:, 0:1], b=el.ap[:, off:QC], s=(kt == ktmax - 1),
                         t_=dtp: e.matmul(o, a, b, start=False, stop=s, tile_position=t_),
                         reads=[onescol16, el], writes=[PSD])
                    vsl = slice(h * 64, h * 64 + dh) if layer == 0 else slice(0, 128)
                    tp = None if (layer == 1 or h == 0) else (0, 64)
                    outsl = PS[2].ap[h * 64:h * 64 + dh, off:QC] if layer == 0 \
                        else PS[2].ap[:, off:QC]
                    p.do("tensor", lambda e, o=outsl, a=VH[kt].ap[:, vsl],
                         b=eh.ap[:, off:QC], s=(kt == 0), t_=tp:
                         e.matmul(o, a, b, start=s, stop=False, tile_position=t_),
                         reads=[VH[kt], eh], writes=[PS[2]])
                    p.do("tensor", lambda e, o=outsl, a=VH[kt].ap[:, vsl],
                         b=el.ap[:, off:QC], t_=tp:
                         e.matmul(o, a, b, start=False, stop=False, tile_position=t_),
                         reads=[VH[kt], el], writes=[PS[2]])
                    p.do("tensor", lambda e, o=outsl, a=VL[kt].ap[:, vsl],
                         b=eh.ap[:, off:QC], s=(kt == ktmax - 1), t_=tp:
                         e.matmul(o, a, b, start=False, stop=s, tile_position=t_),
                         reads=[VL[kt], eh], writes=[PS[2]])
            for h in range(nheads):
                di = h * 32
                drsl = slice(0, 1) if h == 0 else slice(32, 33)
                p.do("vector", lambda e, i=PSD.ap[di:di + 1, :], o=drow_b.ap[drsl, :]:
                     e.reciprocal(out=o, in_=i), reads=[PSD], writes=[drow_b])
                split_hi_lo(drow_b, drow_b.ap[drsl, :], drh_b, drh_b.ap[drsl, :],
                            drl_b, drl_b.ap[drsl, :])
                osl = PS[3].ap[h * 64:h * 64 + 64, :] if layer == 0 else PS[3].ap[:, :]
                tp = None if h == 0 else (32, 64)
                m = 64 if layer == 0 else 128
                ogr = onesrow16b.ap[0:1, 0:m] if h == 0 else onesrow16b.ap[32:33, 0:m]
                p.do("tensor", lambda e, o=osl, a=ogr, b=drh_b.ap[drsl, :], t_=tp:
                     e.matmul(o, a, b, start=True, stop=False, tile_position=t_),
                     reads=[onesrow16b, drh_b], writes=[PS[3]])
                p.do("tensor", lambda e, o=osl, a=ogr, b=drl_b.ap[drsl, :], t_=tp:
                     e.matmul(o, a, b, start=False, stop=True, tile_position=t_),
                     reads=[onesrow16b, drl_b], writes=[PS[3]])
            p.do("scalar", lambda e: e.activation(out=bc32.ap, in_=PS[3].ap[:, :],
                                                  func=AF.Copy),
                 reads=[PS[3]], writes=[bc32])
            p.do("vector", lambda e: e.tensor_tensor(
                out=SC[2].ap, in0=PS[2].ap[:, :], in1=bc32.ap, op=ALU.mult),
                reads=[PS[2], bc32], writes=[SC[2]])
            split_hi_lo(SC[2], SC[2].ap, AH, AH.ap, AL, AL.ap)
            if dbg:
                p.do("vector", lambda e, o=X[qc].ap[:, 0:QC], i=SC[2].ap:
                     e.tensor_copy(out=o, in_=i), reads=[SC[2]], writes=[X[qc]])
                p.do("vector", lambda e, o=X[4 + qc].ap[:, 0:QC], i=bc32.ap:
                     e.tensor_copy(out=o, in_=i), reads=[bc32], writes=[X[4 + qc]])
                continue
            for j in range(QC // 128):
                qt = qc * (QC // 128) + j
                for og in range(2):
                    ps = PS[og]
                    mm3(ps, ps.ap[:, :],
                        AH, AH.ap[:, j * 128:(j + 1) * 128],
                        AL, AL.ap[:, j * 128:(j + 1) * 128],
                        WBH, WBH.ap[:, og * QC:(og + 1) * QC],
                        WBL, WBL.ap[:, og * QC:(og + 1) * QC],
                        start=True, stop=True)
                    p.do("vector", lambda e, i=ps.ap[:, :],
                         o=X[qt].ap[:, og * QC:(og + 1) * QC]: e.scalar_tensor_tensor(
                        out=o, in0=i, scalar=1.0, in1=o, op0=ALU.mult, op1=ALU.add),
                        reads=[ps, X[qt]], writes=[X[qt]])

    def l0_attention():
        build_xt()
        wslots = [Buf(WS32.ap[:, 0:384], "ws0"), Buf(WS32.ap[:, 384:768], "ws1")]
        woslots = [Buf(WS32.ap[:, 768:1280], "wo0"), Buf(WS32.ap[:, 1280:1792], "wo1")]
        for hp in range(8):
            p.dma_s(lambda e, o=bcol.ap[:, 0:3], i=d_bqkv0[hp]:
                 e.dma_start(out=o, in_=i), writes=[bcol])
            for c in range(NDC):
                slb = wslots[c % 2]
                p.dma_s(lambda e, o=slb.ap, i=d_wqkv0[hp, c]:
                     e.dma_start(out=o, in_=i), writes=[slb])
                split_hi_lo(slb, slb.ap, WAH, WAH.ap[:, c * 384:(c + 1) * 384],
                            WAL, WAL.ap[:, c * 384:(c + 1) * 384])
            for half in range(2):
                slb = woslots[half]
                p.dma_s(lambda e, o=slb.ap, i=d_wo0[hp][:, half * 512:(half + 1) * 512]:
                     e.dma_start(out=o, in_=i), writes=[slb])
                split_hi_lo(slb, slb.ap, WBH, WBH.ap[:, half * 512:(half + 1) * 512],
                            WBL, WBL.ap[:, half * 512:(half + 1) * 512])
            for pi, (dsth, dstl) in enumerate(((QH, QL), (KH, KL))):
                for tc in range(NQC):
                    ps = PS[pi]
                    for c in range(NDC):
                        mm3(ps, ps.ap[:, :],
                            WAH, WAH.ap[:, c * 384 + pi * 128:c * 384 + (pi + 1) * 128],
                            WAL, WAL.ap[:, c * 384 + pi * 128:c * 384 + (pi + 1) * 128],
                            XTH[c], XTH[c].ap[:, tc * QC:(tc + 1) * QC],
                            XTL[c], XTL[c].ap[:, tc * QC:(tc + 1) * QC],
                            start=(c == 0), stop=(c == NDC - 1))
                    e32 = SC[2]
                    p.do("scalar", lambda e, i=ps.ap[:, :], o=e32.ap, b=bcol.ap[:, pi:pi + 1]:
                         e.activation(out=o, in_=i, func=AF.Identity, bias=b),
                         reads=[ps, bcol], writes=[e32])
                    p.do("vector", lambda e, i=e32.ap: e.tensor_reduce(
                        out=stat.ap[:, 0:1], in_=i, axis=AX, op=ALU.max,
                        apply_absolute_value=True), reads=[e32], writes=[stat])
                    if tc == 0:
                        p.do("vector", lambda e, s2=2 + pi: e.tensor_copy(
                            out=scal.ap[:, s2:s2 + 1], in_=stat.ap[:, 0:1]),
                            reads=[stat], writes=[scal])
                    else:
                        p.do("vector", lambda e, s2=2 + pi: e.tensor_tensor(
                            out=scal.ap[:, s2:s2 + 1], in0=scal.ap[:, s2:s2 + 1],
                            in1=stat.ap[:, 0:1], op=ALU.max), reads=[stat, scal],
                            writes=[scal])
                    split_hi_lo(e32, e32.ap, dsth, dsth.ap[:, tc * QC:(tc + 1) * QC],
                                dstl, dstl.ap[:, tc * QC:(tc + 1) * QC])
            t1 = p.dma_g(lambda e: e.dma_start(out=d_sc[:, 0:2], in_=scal.ap[:, 2:4]),
                         reads=[scal])
            p.dma_g(lambda e: e.dma_start(
                out=SC[1].ap[0:1, 0:128], in_=d_sc[:, 0:1].transpose([1, 0]),
                allow_slow_non_contiguous=True), writes=[SC[1]], extra=[t1])
            p.dma_g(lambda e: e.dma_start(
                out=SC[1].ap[0:1, 128:256], in_=d_sc[:, 1:2].transpose([1, 0]),
                allow_slow_non_contiguous=True), writes=[SC[1]], extra=[t1])
            p.do("vector", lambda e: e.tensor_reduce(
                out=scal.ap[0:1, 5:6], in_=SC[1].ap[0:1, 0:128], axis=AX, op=ALU.max),
                reads=[SC[1]], writes=[scal])
            p.do("vector", lambda e: e.tensor_reduce(
                out=scal.ap[0:1, 6:7], in_=SC[1].ap[0:1, 128:256], axis=AX, op=ALU.max),
                reads=[SC[1]], writes=[scal])
            p.do("vector", lambda e: e.tensor_tensor(
                out=scal.ap[0:1, 4:5], in0=scal.ap[0:1, 5:6], in1=scal.ap[0:1, 6:7],
                op=ALU.mult), reads=[scal], writes=[scal])
            p.do("vector", lambda e: e.tensor_scalar(
                out=scal.ap[0:1, 4:5], in0=scal.ap[0:1, 4:5], scalar1=-8.0, scalar2=None,
                op0=ALU.mult), reads=[scal], writes=[scal])
            bcast_scalar(scal, scal.ap[0:1, 4:5], negM, negM.ap, 0)
            for tc in range(NQC):
                ps = PS[2]
                for c in range(NDC):
                    mm3(ps, ps.ap[:, :],
                        WAH, WAH.ap[:, c * 384 + 256:c * 384 + 384],
                        WAL, WAL.ap[:, c * 384 + 256:c * 384 + 384],
                        XTH[c], XTH[c].ap[:, tc * QC:(tc + 1) * QC],
                        XTL[c], XTL[c].ap[:, tc * QC:(tc + 1) * QC],
                        start=(c == 0), stop=(c == NDC - 1))
                e32 = SC[2]
                p.do("scalar", lambda e, i=ps.ap[:, :], o=e32.ap, b=bcol.ap[:, 2:3]:
                     e.activation(out=o, in_=i, func=AF.Identity, bias=b),
                     reads=[ps, bcol], writes=[e32])
                split_hi_lo(e32, e32.ap, EH[0], EH[0].ap[:, :], EL[0], EL[0].ap[:, :])
                for j in range(QC // 128):
                    kt = tc * 4 + j
                    for src, dst in ((EH[0], VH[kt]), (EL[0], VL[kt])):
                        pt = PT[j % 2]
                        p.do("tensor", lambda e, i=src.ap[:, j * 128:(j + 1) * 128],
                             o=pt.ap[:, :]: e.transpose(o, i, ident.ap[:, :]),
                             reads=[src, ident], writes=[pt])
                        p.do("vector", lambda e, i=pt.ap[:, :], o=dst.ap:
                             e.tensor_copy(out=o, in_=i), reads=[pt], writes=[dst])
            attn_inner(causal=False, layer=0)
        p.dma_g(lambda e: e.dma_start(out=brow.ap,
                                      in_=d_bo0.partition_broadcast(128)),
                writes=[brow])
        for qt in range(NQT):
            p.do("vector", lambda e, i=X[qt].ap[:, :]: e.tensor_tensor(
                out=i, in0=i, in1=brow.ap, op=ALU.add), reads=[X[qt], brow], writes=[X[qt]])
        layernorm_inplace()

    l0_attention()
    if debug_stop == "p1":
        store_out()
        with es, nc.Block() as block:
            p.emit(block)
        return nc

    dcc_in = Buf(d_ccin, "dcc_in")
    dcc_out = Buf(d_ccout, "dcc_out")
    dsc = Buf(d_sc, "dsc")
    b_w1q = Buf(d_w1q, "b_w1q")
    b_w2q = Buf(d_w2q, "b_w2q")
    b_hq = Buf(d_hq, "b_hq")
    b_hhi = Buf(d_hhi, "b_hhi")
    b_hlo = Buf(d_hlo, "b_hlo")

    def pmax_roundtrip(src_col_ap, src_buf, dst_slot):
        t1 = p.dma_g(lambda e, i=src_col_ap: e.dma_start(
            out=d_sc[:, 4:5], in_=i, allow_slow_non_contiguous=True),
            reads=[src_buf], writes=[dsc])
        p.dma_g(lambda e: e.dma_start(
            out=SC[1].ap[0:1, 0:128], in_=d_sc[:, 4:5].transpose([1, 0]),
            allow_slow_non_contiguous=True), reads=[dsc], writes=[SC[1]], extra=[t1])
        p.do("vector", lambda e, sl2=dst_slot: e.tensor_reduce(
            out=qsc.ap[0:1, sl2:sl2 + 1], in_=SC[1].ap[0:1, 0:128], axis=AX, op=ALU.max),
            reads=[SC[1]], writes=[qsc])

    def allreduce_max_slot(slot):
        t0 = p.do("vector", lambda e, sl2=slot: e.tensor_scalar(
            out=SC[1].ap[0:1, 0:8], in0=ones32.ap[0:1, 0:8],
            scalar1=qsc.ap[0:1, sl2:sl2 + 1], scalar2=None, op0=ALU.mult),
            reads=[ones32, qsc], writes=[SC[1]])
        t1 = p.dma_g(lambda e: e.dma_start(
            out=d_ccin[:, :], in_=SC[1].ap[0:1, 0:8]),
            reads=[SC[1]], writes=[dcc_in])
        t2 = p.do("gpsimd", lambda e: e.collective_compute(
            "AllReduce", ALU.max, replica_groups=[list(range(8))],
            ins=[d_ccin], outs=[d_ccout]), reads=[dcc_in], writes=[dcc_out], extra=[t1])
        p.dma_g(lambda e, sl2=slot: e.dma_start(
            out=qsc.ap[:, sl2:sl2 + 1],
            in_=d_ccout[0:1, 0:1].partition_broadcast(128)),
            reads=[dcc_out], writes=[qsc])

    def bcast_slot(slot):
        t1 = p.dma_g(lambda e, sl2=slot: e.dma_start(
            out=d_sc2[0:1, 1:2], in_=qsc.ap[0:1, sl2:sl2 + 1]), reads=[qsc])
        p.dma_g(lambda e, sl2=slot: e.dma_start(
            out=qsc.ap[:, sl2:sl2 + 1], in_=d_sc2[0:1, 1:2].partition_broadcast(128)),
            writes=[qsc], extra=[t1])

    def make_scale(raw_slot, scale_slot, inv_slot):
        p.do("vector", lambda e, a=raw_slot, b=scale_slot: e.tensor_scalar(
            out=qsc.ap[:, b:b + 1], in0=qsc.ap[:, a:a + 1], scalar1=1e-6,
            scalar2=1.0 / 127.0, op0=ALU.max, op1=ALU.mult), reads=[qsc], writes=[qsc])
        p.do("vector", lambda e, b=scale_slot, c=inv_slot: e.reciprocal(
            out=qsc.ap[:, c:c + 1], in_=qsc.ap[:, b:b + 1]), reads=[qsc], writes=[qsc])

    def dual_round_to(tb, t_ap, outb, out_ap, s1b, s1ap, s2b, s2ap):
        p.do("vector", lambda e, i=t_ap, o=s1ap: e.tensor_scalar(
            out=o, in0=i, scalar1=C23, scalar2=C23, op0=ALU.add, op1=ALU.subtract),
            reads=[tb], writes=[s1b])
        p.do("vector", lambda e, i=t_ap, o=s2ap: e.tensor_scalar(
            out=o, in0=i, scalar1=C52, scalar2=C52, op0=ALU.add, op1=ALU.subtract),
            reads=[tb], writes=[s2b])
        p.do("vector", lambda e, i0=s1ap, i1=s2ap, o=out_ap: e.scalar_tensor_tensor(
            out=o, in0=i0, scalar=delta.ap, in1=i1, op0=ALU.mult, op1=ALU.add),
            reads=[s1b, s2b, delta], writes=[outb])

    def max_accum(src_ap, src_bufs, slot, first):
        p.do("vector", lambda e, i=src_ap: e.tensor_reduce(
            out=stat.ap[:, 0:1], in_=i, axis=AX, op=ALU.max,
            apply_absolute_value=True), reads=src_bufs, writes=[stat])
        if first:
            p.do("vector", lambda e, sl2=slot: e.tensor_copy(
                out=qsc.ap[:, sl2:sl2 + 1], in_=stat.ap[:, 0:1]),
                reads=[stat], writes=[qsc])
        else:
            p.do("vector", lambda e, sl2=slot: e.tensor_tensor(
                out=qsc.ap[:, sl2:sl2 + 1], in0=qsc.ap[:, sl2:sl2 + 1],
                in1=stat.ap[:, 0:1], op=ALU.max), reads=[stat], writes=[qsc])

    def l0_ffn():
        p.barrier()
        c2a = _Carve(AR16, 24704)
        XQ = c2a.take(1024, name="XQ")
        W1Q = [c2a.take(4096, name="W1Qa"), c2a.take(4096, name="W1Qb")]
        HHr = [c2a.take(QC, name=f"HH{i}") for i in range(4)]
        HLr = [c2a.take(QC, name=f"HL{i}") for i in range(4)]
        HST = c2a.take(2048, name="HST")
        HLO = c2a.take(2048, name="HLO")
        for qt in range(NQT):
            max_accum(X[qt].ap[:, :], [X[qt]], 0, qt == 0)
        pmax_roundtrip(qsc.ap[:, 0:1], qsc, 0)
        allreduce_max_slot(0)
        make_scale(0, 1, 2)
        for qt in range(NQT):
            for hf in range(2):
                p.do("vector", lambda e, i=X[qt].ap[:, hf * QC:(hf + 1) * QC], o=SC[2].ap:
                     e.tensor_scalar(out=o, in0=i, scalar1=qsc.ap[:, 2:3], scalar2=None,
                                     op0=ALU.mult), reads=[X[qt], qsc], writes=[SC[2]])
                dual_round_to(SC[2], SC[2].ap, XQ, XQ.ap[:, hf * QC:(hf + 1) * QC],
                              SC[0], SC[0].ap, SC[1], SC[1].ap)
            for cc in range(NDC):
                pt = PT[cc % 2]
                p.do("tensor", lambda e, i=XQ.ap[:, cc * 128:(cc + 1) * 128], o=pt.ap[:, :]:
                     e.transpose(o, i, ident.ap[:, :]), reads=[XQ, ident], writes=[pt])
                p.do("vector", lambda e, i=pt.ap[:, :],
                     o=XTH[cc].ap[:, qt * 128:(qt + 1) * 128]:
                     e.tensor_copy(out=o, in_=i), reads=[pt], writes=[XTH[cc]])
        for c_ in range(NDC):
            for hf in range(2):
                p.dma_s(lambda e, o=WS32.ap[:, :],
                     i=d_w1t[c_][:, hf * 2048:(hf + 1) * 2048]:
                     e.dma_start(out=o, in_=i), writes=[WS32])
                max_accum(WS32.ap[:, :], [WS32], 3, c_ == 0 and hf == 0)
        for c2 in range(32):
            p.dma_s(lambda e, o=WS32.ap[:, 0:1024], i=d_w2t[c2]:
                 e.dma_start(out=o, in_=i), writes=[WS32])
            max_accum(WS32.ap[:, 0:1024], [WS32], 6, c2 == 0)
        pmax_roundtrip(qsc.ap[:, 3:4], qsc, 3)
        bcast_slot(3)
        make_scale(3, 4, 5)
        pmax_roundtrip(qsc.ap[:, 6:7], qsc, 6)
        bcast_slot(6)
        make_scale(6, 7, 8)
        p.do("vector", lambda e: e.tensor_tensor(
            out=qsc.ap[:, 9:10], in0=qsc.ap[:, 1:2], in1=qsc.ap[:, 4:5], op=ALU.mult),
            reads=[qsc], writes=[qsc])
        for c_ in range(NDC):
            wq16 = W1Q[c_ % 2]
            for hf in range(2):
                p.dma_s(lambda e, o=WS32.ap[:, :],
                     i=d_w1t[c_][:, hf * 2048:(hf + 1) * 2048]:
                     e.dma_start(out=o, in_=i), writes=[WS32])
                for pc in range(2):
                    p.do("vector", lambda e, i=WS32.ap[:, pc * 1024:(pc + 1) * 1024],
                         o=h32.ap[:, 0:1024]: e.tensor_scalar(
                        out=o, in0=i, scalar1=qsc.ap[:, 5:6], scalar2=None,
                        op0=ALU.mult), reads=[WS32, qsc], writes=[h32])
                    dual_round_to(h32, h32.ap[:, 0:1024], wq16,
                                  wq16.ap[:, hf * 2048 + pc * 1024:
                                          hf * 2048 + (pc + 1) * 1024],
                                  h32, h32.ap[:, 1024:2048], brow, brow.ap)
            p.dma_g(lambda e, i=wq16.ap, o=d_w1q[c_]: e.dma_start(out=o, in_=i),
                    reads=[wq16], writes=[b_w1q])
        for c2 in range(32):
            wq16 = W1Q[c2 % 2]
            p.dma_s(lambda e, o=WS32.ap[:, 0:1024], i=d_w2t[c2]:
                 e.dma_start(out=o, in_=i), writes=[WS32])
            p.do("vector", lambda e, i=WS32.ap[:, 0:1024], o=h32.ap[:, 0:1024]:
                 e.tensor_scalar(out=o, in0=i, scalar1=qsc.ap[:, 8:9], scalar2=None,
                                 op0=ALU.mult), reads=[WS32, qsc], writes=[h32])
            dual_round_to(h32, h32.ap[:, 0:1024], wq16, wq16.ap[:, 0:1024],
                          h32, h32.ap[:, 1024:2048], brow, brow.ap)
            p.dma_g(lambda e, i=wq16.ap[:, 0:1024], o=d_w2q[c2]:
                    e.dma_start(out=o, in_=i), reads=[wq16], writes=[b_w2q])
        p.flush_g([b_w1q, b_w2q])
        p.dma_s(lambda e: e.dma_start(out=gates.ap[:, 0:32], in_=d_b1),
             writes=[gates])
        p.do("vector", lambda e: e.tensor_scalar(
            out=gates.ap[:, 0:32], in0=gates.ap[:, 0:32], scalar1=qsc.ap[:, 9:10],
            scalar2=None, op0=ALU.mult), reads=[gates, qsc], writes=[gates])
        for grp in range(8):
            wq16 = W1Q[grp % 2]
            p.dma_s(lambda e, o=wq16.ap,
                 i=d_w1q[:, :, grp * 512:(grp + 1) * 512].transpose([1, 0, 2]):
                 e.dma_start(out=o, in_=i), reads=[b_w1q], writes=[wq16])
            for od4 in range(4):
                odg = grp * 4 + od4
                for tc in range(NQC):
                    ps = PS[(od4 * NQC + tc) % 2]
                    for c_ in range(NDC):
                        p.do("tensor", lambda e, o=ps.ap[:, :],
                             a=wq16.ap[:, c_ * 512 + od4 * 128:c_ * 512 + (od4 + 1) * 128],
                             b=XTH[c_].ap[:, tc * QC:(tc + 1) * QC], s0=(c_ == 0),
                             s1=(c_ == NDC - 1):
                             e.matmul(o, a, b, start=s0, stop=s1),
                             reads=[wq16, XTH[c_]], writes=[ps])
                    p.do("scalar", lambda e, i=ps.ap[:, :], o=SC[2].ap,
                         b=gates.ap[:, odg:odg + 1]: e.activation(
                        out=o, in_=i, func=AF.Gelu, bias=b, scale=qsc.ap[:, 9:10]),
                        reads=[ps, gates, qsc], writes=[SC[2]])
                    max_accum(SC[2].ap, [SC[2]], 10, grp == 0 and od4 == 0 and tc == 0)
                    HH = HHr[(od4 * NQC + tc) % 4]
                    HL = HLr[(od4 * NQC + tc) % 4]
                    split_hi_lo(SC[2], SC[2].ap, HH, HH.ap, HL, HL.ap)
                    p.do("gpsimd", lambda e, i=HH.ap,
                         o=d_hhi[odg * 128:(odg + 1) * 128, tc * QC:(tc + 1) * QC]:
                         e.dma_start(out=o, in_=i), reads=[HH], writes=[b_hhi], dma=True)
                    p.do("gpsimd", lambda e, i=HL.ap,
                         o=d_hlo[odg * 128:(odg + 1) * 128, tc * QC:(tc + 1) * QC]:
                         e.dma_start(out=o, in_=i), reads=[HL], writes=[b_hlo], dma=True)
        pmax_roundtrip(qsc.ap[:, 10:11], qsc, 10)
        allreduce_max_slot(10)
        make_scale(10, 11, 0)
        p.do("vector", lambda e: e.tensor_tensor(
            out=qsc.ap[:, 2:3], in0=qsc.ap[:, 11:12], in1=qsc.ap[:, 7:8], op=ALU.mult),
            reads=[qsc], writes=[qsc])
        for c2 in range(32):
            p.dma_g(lambda e, o=h32.ap, i=d_hhi[c2 * 128:(c2 + 1) * 128, :]:
                    e.dma_start(out=o, in_=i), reads=[b_hhi], writes=[h32])
            p.dma_g(lambda e, o=HLO.ap, i=d_hlo[c2 * 128:(c2 + 1) * 128, :]:
                    e.dma_start(out=o, in_=i), reads=[b_hlo], writes=[HLO])
            p.do("vector", lambda e: e.tensor_tensor(
                out=h32.ap, in0=h32.ap, in1=HLO.ap, op=ALU.add),
                reads=[h32, HLO], writes=[h32])
            for pc in range(4):
                p.do("vector", lambda e, i=h32.ap[:, pc * QC:(pc + 1) * QC]:
                     e.tensor_scalar(out=i, in0=i, scalar1=qsc.ap[:, 0:1], scalar2=None,
                                     op0=ALU.mult), reads=[h32, qsc], writes=[h32])
                dual_round_to(h32, h32.ap[:, pc * QC:(pc + 1) * QC], HST,
                              HST.ap[:, pc * QC:(pc + 1) * QC],
                              SC[0], SC[0].ap, SC[1], SC[1].ap)
            p.dma_g(lambda e, i=HST.ap, o=d_hq[c2]: e.dma_start(out=o, in_=i),
                    reads=[HST], writes=[b_hq])
        if debug_stop == "p2sc":
            p.do("vector", lambda e: e.tensor_copy(out=X[0].ap[:, 0:12], in_=qsc.ap[:, :]),
                 reads=[qsc], writes=[X[0]])
            return
        p.flush_g([b_hq])
        p.barrier()
        c2b = _Carve(AR16, 24704)
        W2R = c2b.take(16384, name="W2R")
        HQG = [c2b.take(4096, name="HQGa"), c2b.take(4096, name="HQGb")]
        p.dma_g(lambda e: e.dma_start(out=brow.ap,
                                      in_=d_b2.partition_broadcast(128)),
                writes=[brow])
        p.do("vector", lambda e: e.tensor_scalar(
            out=brow.ap, in0=brow.ap, scalar1=qsc.ap[:, 2:3], scalar2=None,
            op0=ALU.mult), reads=[brow, qsc], writes=[brow])
        for qt in range(NQT):
            p.do("vector", lambda e, i=X[qt].ap[:, :]: e.tensor_tensor(
                out=i, in0=i, in1=brow.ap, op=ALU.add), reads=[X[qt], brow],
                writes=[X[qt]])
        for og2 in range(2):
            for gq in range(4):
                p.dma_s(lambda e,
                     o=W2R.ap[:, gq * 4096:(gq + 1) * 4096],
                     i=d_w2q[gq * 8:(gq + 1) * 8, :, og2 * 512:(og2 + 1) * 512]
                     .transpose([1, 0, 2]):
                     e.dma_start(out=o, in_=i), reads=[b_w2q], writes=[W2R])
            for qt in range(NQT):
                hq16 = HQG[qt % 2]
                p.dma_s(lambda e, o=hq16.ap,
                     i=d_hq[:, :, qt * 128:(qt + 1) * 128].transpose([1, 0, 2]):
                     e.dma_start(out=o, in_=i), reads=[b_hq], writes=[hq16])
                ps = PS[qt % 4]
                for c2 in range(32):
                    p.do("tensor", lambda e, o=ps.ap[:, :],
                         a=hq16.ap[:, c2 * 128:(c2 + 1) * 128],
                         b=W2R.ap[:, c2 * 512:(c2 + 1) * 512], s0=(c2 == 0),
                         s1=(c2 == 31): e.matmul(o, a, b, start=s0, stop=s1),
                         reads=[hq16, W2R], writes=[ps])
                p.do("vector", lambda e, i=ps.ap[:, :],
                     o=X[qt].ap[:, og2 * QC:(og2 + 1) * QC]: e.scalar_tensor_tensor(
                    out=o, in0=i, scalar=qsc.ap[:, 2:3], in1=o, op0=ALU.mult,
                    op1=ALU.add), reads=[ps, X[qt], qsc], writes=[X[qt]])
        layernorm_inplace()

    l0_ffn()
    if debug_stop in ("p2", "p2sc"):
        store_out()
        with es, nc.Block() as block:
            p.emit(block)
        return nc

    def l1_attention():
        p.barrier()
        p.do("vector", lambda e: e.memset(onesrow16b.ap[:, :], 1.0), writes=[onesrow16b])
        build_xt()
        p.dma_g(lambda e: e.dma_start(out=SC[0].ap[:, 0:64],
                                      in_=d_jrow.partition_broadcast(128)),
                writes=[SC[0]])
        p.do("vector", lambda e: e.tensor_scalar(
            out=SC[0].ap[:, 64:128], in0=SC[0].ap[:, 0:64], scalar1=-C_LN_HI,
            scalar2=None, op0=ALU.mult), reads=[SC[0]], writes=[SC[0]])
        p.do("vector", lambda e: e.scalar_tensor_tensor(
            out=SC[0].ap[:, 128:192], in0=SC[0].ap[:, 0:64], scalar=-C_LN_LO,
            in1=SC[0].ap[:, 64:128], op0=ALU.mult, op1=ALU.add),
            reads=[SC[0]], writes=[SC[0]])
        p.do("scalar", lambda e: e.activation(
            out=SC[0].ap[:, 0:64], in_=SC[0].ap[:, 128:192], func=AF.Exp),
            reads=[SC[0]], writes=[SC[0]])
        for qt in range(NQT):
            p.do("gpsimd", lambda e, b=qt * 128: e.iota(
                scal.ap[:, 14:15], pattern=[[0, 1]], base=b, channel_multiplier=1,
                allow_small_or_imprecise_dtypes=True), writes=[scal])
            for trig in range(2):
                p.do("vector", lambda e: e.tensor_scalar(
                    out=SC[1].ap[:, 0:64], in0=SC[0].ap[:, 0:64],
                    scalar1=scal.ap[:, 14:15], scalar2=None, op0=ALU.mult),
                    reads=[SC[0], scal], writes=[SC[1]])
                if trig == 1:
                    p.do("vector", lambda e: e.tensor_scalar(
                        out=SC[1].ap[:, 0:64], in0=SC[1].ap[:, 0:64], scalar1=HALF_PI,
                        scalar2=None, op0=ALU.add), reads=[SC[1]], writes=[SC[1]])
                p.do("vector", lambda e: e.tensor_scalar(
                    out=SC[1].ap[:, 64:128], in0=SC[1].ap[:, 0:64], scalar1=INV_2PI,
                    scalar2=None, op0=ALU.mult), reads=[SC[1]], writes=[SC[1]])
                dual_round_to(SC[1], SC[1].ap[:, 64:128], SC[1], SC[1].ap[:, 64:128],
                              SC[1], SC[1].ap[:, 128:192], SC[1], SC[1].ap[:, 192:256])
                p.do("vector", lambda e: e.scalar_tensor_tensor(
                    out=SC[1].ap[:, 128:192], in0=SC[1].ap[:, 64:128],
                    scalar=-C2PI_HI, in1=SC[1].ap[:, 0:64], op0=ALU.mult, op1=ALU.add),
                    reads=[SC[1]], writes=[SC[1]])
                p.do("vector", lambda e: e.scalar_tensor_tensor(
                    out=SC[1].ap[:, 192:256], in0=SC[1].ap[:, 64:128],
                    scalar=-C2PI_MID, in1=SC[1].ap[:, 128:192], op0=ALU.mult,
                    op1=ALU.add), reads=[SC[1]], writes=[SC[1]])
                dst = SIN[qt] if trig == 0 else COS[qt]
                p.do("scalar", lambda e, o=dst.ap: e.activation(
                    out=o, in_=SC[1].ap[:, 192:256], func=AF.Sin),
                    reads=[SC[1]], writes=[dst])
        if debug_stop == "p3tab":
            for qt in range(NQT):
                p.do("vector", lambda e, o=X[qt].ap[:, 0:64], i=COS[qt].ap:
                     e.tensor_copy(out=o, in_=i), reads=[COS[qt]], writes=[X[qt]])
                p.do("vector", lambda e, o=X[qt].ap[:, 64:128], i=SIN[qt].ap:
                     e.tensor_copy(out=o, in_=i), reads=[SIN[qt]], writes=[X[qt]])
            return
        wslots = [Buf(WS32.ap[:, 0:384], "ws0"), Buf(WS32.ap[:, 384:768], "ws1")]
        woslots = [Buf(WS32.ap[:, 768:1280], "wo0"), Buf(WS32.ap[:, 1280:1792], "wo1")]
        for h in range(8):
            for c_ in range(NDC):
                slb = wslots[c_ % 2]
                p.dma_s(lambda e, o=slb.ap, i=d_wqkv1[h, c_]:
                     e.dma_start(out=o, in_=i), writes=[slb])
                split_hi_lo(slb, slb.ap, WAH, WAH.ap[:, c_ * 384:(c_ + 1) * 384],
                            WAL, WAL.ap[:, c_ * 384:(c_ + 1) * 384])
            for half in range(2):
                slb = woslots[half]
                p.dma_s(lambda e, o=slb.ap,
                     i=d_wo1[h][:, half * 512:(half + 1) * 512]:
                     e.dma_start(out=o, in_=i), writes=[slb])
                split_hi_lo(slb, slb.ap, WBH, WBH.ap[:, half * 512:(half + 1) * 512],
                            WBL, WBL.ap[:, half * 512:(half + 1) * 512])
            for qt in range(NQT):
                ps = PS[qt % 2]
                for c_ in range(NDC):
                    mm3(ps, ps.ap[:, 0:384],
                        XTH[c_], XTH[c_].ap[:, qt * 128:(qt + 1) * 128],
                        XTL[c_], XTL[c_].ap[:, qt * 128:(qt + 1) * 128],
                        WAH, WAH.ap[:, c_ * 384:(c_ + 1) * 384],
                        WAL, WAL.ap[:, c_ * 384:(c_ + 1) * 384],
                        start=(c_ == 0), stop=(c_ == NDC - 1))
                p.do("scalar", lambda e, i=ps.ap[:, 0:384], o=SC[2].ap[:, 0:384]:
                     e.activation(out=o, in_=i, func=AF.Copy),
                     reads=[ps], writes=[SC[2]])
                split_hi_lo(SC[2], SC[2].ap[:, 256:384], VH[qt], VH[qt].ap,
                            VL[qt], VL[qt].ap)
                for which, (dsth, dstl) in enumerate(((QH, QL), (KH, KL))):
                    sl_ap = SC[2].ap[:, which * 128:(which + 1) * 128]
                    p.do("scalar", lambda e, i=sl_ap: e.activation(
                        out=SC[1].ap[:, 0:128], in_=i, func=AF.Square,
                        accum_out=stat.ap[:, 0:1]), reads=[SC[2]], writes=[SC[1], stat])
                    p.do("vector", lambda e: e.tensor_scalar(
                        out=stat.ap[:, 1:2], in0=stat.ap[:, 0:1], scalar1=1.0 / 128.0,
                        scalar2=None, op0=ALU.mult), reads=[stat], writes=[stat])
                    p.do("scalar", lambda e: e.activation(
                        out=stat.ap[:, 2:3], in_=stat.ap[:, 1:2], func=AF.Sqrt,
                        bias=1e-6), reads=[stat, const32], writes=[stat])
                    p.do("vector", lambda e: e.reciprocal(
                        out=stat.ap[:, 3:4], in_=stat.ap[:, 2:3]),
                        reads=[stat], writes=[stat])
                    p.do("vector", lambda e, i=sl_ap: e.tensor_scalar(
                        out=SC[1].ap[:, 0:128], in0=i, scalar1=stat.ap[:, 3:4],
                        scalar2=None, op0=ALU.mult), reads=[SC[2], stat],
                        writes=[SC[1]])
                    p.do("vector", lambda e, c=COS[qt]: e.tensor_tensor(
                        out=SC[1].ap[:, 128:192], in0=SC[1].ap[:, 0:64], in1=c.ap,
                        op=ALU.mult), reads=[SC[1], COS[qt]], writes=[SC[1]])
                    p.do("vector", lambda e, s=SIN[qt]: e.tensor_tensor(
                        out=SC[1].ap[:, 192:256], in0=SC[1].ap[:, 64:128], in1=s.ap,
                        op=ALU.mult), reads=[SC[1], SIN[qt]], writes=[SC[1]])
                    p.do("vector", lambda e: e.tensor_tensor(
                        out=SC[1].ap[:, 128:192], in0=SC[1].ap[:, 128:192],
                        in1=SC[1].ap[:, 192:256], op=ALU.subtract),
                        reads=[SC[1]], writes=[SC[1]])
                    p.do("vector", lambda e, c=COS[qt]: e.tensor_tensor(
                        out=SC[1].ap[:, 192:256], in0=SC[1].ap[:, 64:128], in1=c.ap,
                        op=ALU.mult), reads=[SC[1], COS[qt]], writes=[SC[1]])
                    p.do("vector", lambda e, s=SIN[qt]: e.tensor_tensor(
                        out=SC[1].ap[:, 256:320], in0=SC[1].ap[:, 0:64], in1=s.ap,
                        op=ALU.mult), reads=[SC[1], SIN[qt]], writes=[SC[1]])
                    p.do("vector", lambda e: e.tensor_tensor(
                        out=SC[1].ap[:, 192:256], in0=SC[1].ap[:, 192:256],
                        in1=SC[1].ap[:, 256:320], op=ALU.add),
                        reads=[SC[1]], writes=[SC[1]])
                    split_hi_lo(SC[1], SC[1].ap[:, 128:256], EH[1], EH[1].ap[:, 0:128],
                                EL[1], EL[1].ap[:, 0:128])
                    for half, (src16, dst16) in enumerate(((EH[1], dsth), (EL[1], dstl))):
                        pt = PT[half]
                        p.do("tensor", lambda e, i=src16.ap[:, 0:128], o=pt.ap[:, :]:
                             e.transpose(o, i, ident.ap[:, :]),
                             reads=[src16, ident], writes=[pt])
                        p.do("vector", lambda e, i=pt.ap[:, :],
                             o=dst16.ap[:, qt * 128:(qt + 1) * 128]:
                             e.tensor_copy(out=o, in_=i), reads=[pt], writes=[dst16])
            if debug_stop == "p3qk" and h == 0:
                for half, (s16h, s16l) in enumerate(((QH, QL), (KH, KL))):
                    for pcc in range(2):
                        p.do("vector", lambda e, o=X[half * 2 + pcc].ap[:, :],
                             i0=s16h.ap[:, pcc * 1024:(pcc + 1) * 1024],
                             i1=s16l.ap[:, pcc * 1024:(pcc + 1) * 1024]:
                             e.tensor_tensor(out=o, in0=i0, in1=i1, op=ALU.add),
                             reads=[s16h, s16l], writes=[X[half * 2 + pcc]])
                for kt in range(8):
                    p.do("vector", lambda e, o=X[4].ap[:, kt * 128:kt * 128 + 128],
                         i0=VH[kt].ap, i1=VL[kt].ap: e.tensor_tensor(
                        out=o, in0=i0, in1=i1, op=ALU.add),
                        reads=[VH[kt], VL[kt]], writes=[X[4]])
                return
            attn_inner(causal=True, layer=1, dbg=(debug_stop == "p3a0" and h == 0))
            if debug_stop == "p3a0" and h == 0:
                return
        layernorm_inplace()

    l1_attention()
    if debug_stop in ("p3", "p3tab", "p3qk", "p3a0"):
        store_out()
        with es, nc.Block() as block:
            p.emit(block)
        return nc

    def l1_ffn():
        p.barrier()
        build_xt()
        c4a = _Carve(AR16, 24704)
        GFH = c4a.take(T, name="GFH"); GFL = c4a.take(T, name="GFL")
        WPH = c4a.take(2048, name="WPH"); WPL = c4a.take(2048, name="WPL")
        PJH = [c4a.take(T, name=f"PJH{i}") for i in range(2)]
        PJL = [c4a.take(T, name=f"PJL{i}") for i in range(2)]
        SMH = c4a.take(8, name="SMH"); SML = c4a.take(8, name="SML")
        WDH = c4a.take(128, name="WDH"); WDL = c4a.take(128, name="WDL")
        for c_ in range(NDC):
            p.dma_s(lambda e, o=WS32.ap[:, 0:256], i=d_wpt[c_]:
                 e.dma_start(out=o, in_=i), writes=[WS32])
            split_hi_lo(WS32, WS32.ap[:, 0:256], WPH, WPH.ap[:, c_ * 256:(c_ + 1) * 256],
                        WPL, WPL.ap[:, c_ * 256:(c_ + 1) * 256])
        for c_ in range(NDC):
            p.dma_s(lambda e, o=WS32.ap[:, 0:15], i=d_wdt[c_]:
                 e.dma_start(out=o, in_=i), writes=[WS32])
            split_hi_lo(WS32, WS32.ap[:, 0:15], WDH, WDH.ap[:, c_ * 16:c_ * 16 + 15],
                        WDL, WDL.ap[:, c_ * 16:c_ * 16 + 15])
        p.dma_s(lambda e: e.dma_start(out=scal.ap[:, 8:10], in_=d_bp),
             writes=[bcol])
        p.dma_s(lambda e: e.dma_start(out=SC[2].ap[0:4, 0:256], in_=d_simt),
             writes=[SC[2]])
        p.do("scalar", lambda e: e.activation(
            out=SC[1].ap[0:4, 0:256], in_=SC[2].ap[0:4, 0:256], func=AF.Square,
            accum_out=stat.ap[0:4, 0:1]), reads=[SC[2]], writes=[SC[1], stat])
        p.do("scalar", lambda e: e.activation(
            out=stat.ap[0:4, 1:2], in_=stat.ap[0:4, 0:1], func=AF.Sqrt),
            reads=[stat], writes=[stat])
        p.do("vector", lambda e: e.tensor_scalar(
            out=stat.ap[0:4, 1:2], in0=stat.ap[0:4, 1:2], scalar1=1e-12, scalar2=None,
            op0=ALU.max), reads=[stat], writes=[stat])
        p.do("vector", lambda e: e.reciprocal(out=stat.ap[0:4, 2:3],
                                              in_=stat.ap[0:4, 1:2]),
             reads=[stat], writes=[stat])
        p.dma_g(lambda e: e.dma_start(out=stat.ap[0:1, 3:4], in_=d_temp),
                writes=[stat])
        p.do("vector", lambda e: e.tensor_scalar(
            out=stat.ap[0:1, 3:4], in0=stat.ap[0:1, 3:4], scalar1=CLAMP_MAX,
            scalar2=None, op0=ALU.min), reads=[stat], writes=[stat])
        p.do("scalar", lambda e: e.activation(
            out=stat.ap[0:1, 4:5], in_=stat.ap[0:1, 3:4], func=AF.Exp),
            reads=[stat], writes=[stat])
        t1 = p.dma_g(lambda e: e.dma_start(out=d_sc2[0:1, 2:3],
                                           in_=stat.ap[0:1, 4:5]), reads=[stat])
        p.dma_g(lambda e: e.dma_start(
            out=stat.ap[:, 4:5], in_=d_sc2[0:1, 2:3].partition_broadcast(128)),
            writes=[stat], extra=[t1])
        p.do("vector", lambda e: e.tensor_scalar(
            out=SC[2].ap[0:4, 0:256], in0=SC[2].ap[0:4, 0:256],
            scalar1=stat.ap[0:4, 2:3], scalar2=stat.ap[0:4, 4:5], op0=ALU.mult,
            op1=ALU.mult), reads=[SC[2], stat], writes=[SC[2]])
        split_hi_lo(SC[2], SC[2].ap[0:4, 0:256], EH[0], EH[0].ap[0:4, 0:256],
                    EL[0], EL[0].ap[0:4, 0:256])
        for cc in range(2):
            for half, (src16, dstb) in enumerate(((EH[0], SMH), (EL[0], SML))):
                pt = PT[half]
                p.do("tensor", lambda e, i=src16.ap[0:4, cc * 128:(cc + 1) * 128],
                     o=pt.ap[:, 0:4]: e.transpose(o, i, ident.ap[0:4, 0:4]),
                     reads=[src16, ident], writes=[pt])
                p.do("vector", lambda e, i=pt.ap[:, 0:4], o=dstb.ap[:, cc * 4:cc * 4 + 4]:
                     e.tensor_copy(out=o, in_=i), reads=[pt], writes=[dstb])
        for tc in range(NQC):
            for fc in range(2):
                ps = PS[fc]
                for c_ in range(NDC):
                    mm3(ps, ps.ap[:, :],
                        WPH, WPH.ap[:, c_ * 256 + fc * 128:c_ * 256 + (fc + 1) * 128],
                        WPL, WPL.ap[:, c_ * 256 + fc * 128:c_ * 256 + (fc + 1) * 128],
                        XTH[c_], XTH[c_].ap[:, tc * QC:(tc + 1) * QC],
                        XTL[c_], XTL[c_].ap[:, tc * QC:(tc + 1) * QC],
                        start=(c_ == 0), stop=(c_ == NDC - 1))
                dst32 = SC[0] if fc == 0 else SC[2]
                p.do("scalar", lambda e, i=ps.ap[:, :], o=dst32.ap,
                     b=scal.ap[:, 8 + fc:9 + fc]: e.activation(
                    out=o, in_=i, func=AF.Identity, bias=b),
                    reads=[ps, bcol], writes=[dst32])
                p.do("scalar", lambda e, i=dst32.ap: e.activation(
                    out=SC[1].ap, in_=i, func=AF.Square), reads=[dst32], writes=[SC[1]])
                split_hi_lo(SC[1], SC[1].ap, EH[0], EH[0].ap, EL[0], EL[0].ap)
                p.do("tensor", lambda e, o=PSD.ap[0:1, :], a=onescol16.ap[:, 0:1],
                     b=EH[0].ap, s=(fc == 0): e.matmul(o, a, b, start=s, stop=False),
                     reads=[onescol16, EH[0]], writes=[PSD])
                p.do("tensor", lambda e, o=PSD.ap[0:1, :], a=onescol16.ap[:, 0:1],
                     b=EL[0].ap, s=(fc == 1): e.matmul(o, a, b, start=False, stop=s),
                     reads=[onescol16, EL[0]], writes=[PSD])
            p.do("scalar", lambda e: e.activation(
                out=drow_b.ap[0:1, :], in_=PSD.ap[0:1, :], func=AF.Sqrt),
                reads=[PSD], writes=[drow_b])
            p.do("vector", lambda e: e.tensor_scalar(
                out=drow_b.ap[0:1, :], in0=drow_b.ap[0:1, :], scalar1=1e-12,
                scalar2=None, op0=ALU.max), reads=[drow_b], writes=[drow_b])
            p.do("vector", lambda e: e.reciprocal(out=drow_b.ap[0:1, :],
                                                  in_=drow_b.ap[0:1, :]),
                 reads=[drow_b], writes=[drow_b])
            split_hi_lo(drow_b, drow_b.ap[0:1, :], drh_b, drh_b.ap[0:1, :],
                        drl_b, drl_b.ap[0:1, :])
            p.do("tensor", lambda e, o=PS[3].ap[:, :], a=onesrow16b.ap[0:1, 0:128],
                 b=drh_b.ap[0:1, :]: e.matmul(o, a, b, start=True, stop=False),
                 reads=[onesrow16b, drh_b], writes=[PS[3]])
            p.do("tensor", lambda e, o=PS[3].ap[:, :], a=onesrow16b.ap[0:1, 0:128],
                 b=drl_b.ap[0:1, :]: e.matmul(o, a, b, start=False, stop=True),
                 reads=[onesrow16b, drl_b], writes=[PS[3]])
            p.do("scalar", lambda e: e.activation(out=bc32.ap, in_=PS[3].ap[:, :],
                                                  func=AF.Copy),
                 reads=[PS[3]], writes=[bc32])
            for fc in range(2):
                src32 = SC[0] if fc == 0 else SC[2]
                p.do("vector", lambda e, i=src32.ap: e.tensor_tensor(
                    out=i, in0=i, in1=bc32.ap, op=ALU.mult),
                    reads=[src32, bc32], writes=[src32])
                split_hi_lo(src32, src32.ap, PJH[fc],
                            PJH[fc].ap[:, tc * QC:(tc + 1) * QC],
                            PJL[fc], PJL[fc].ap[:, tc * QC:(tc + 1) * QC])
        for qt in range(NQT):
            ps = PS[2]
            for fc in range(2):
                mm3(ps, ps.ap[:, 0:4],
                    PJH[fc], PJH[fc].ap[:, qt * 128:(qt + 1) * 128],
                    PJL[fc], PJL[fc].ap[:, qt * 128:(qt + 1) * 128],
                    SMH, SMH.ap[:, fc * 4:(fc + 1) * 4],
                    SML, SML.ap[:, fc * 4:(fc + 1) * 4],
                    start=(fc == 0), stop=(fc == 1))
            p.do("scalar", lambda e, i=ps.ap[:, 0:4]: e.activation(
                out=SC[0].ap[:, 0:4], in_=i, func=AF.Copy), reads=[ps], writes=[SC[0]])
            p.do("vector", lambda e: e.tensor_reduce(
                out=stat.ap[:, 0:1], in_=SC[0].ap[:, 0:4], axis=AX, op=ALU.max),
                reads=[SC[0]], writes=[stat])
            p.do("vector", lambda e: e.tensor_scalar(
                out=stat.ap[:, 1:2], in0=stat.ap[:, 0:1], scalar1=-1.0, scalar2=None,
                op0=ALU.mult), reads=[stat], writes=[stat])
            p.do("scalar", lambda e: e.activation(
                out=SC[0].ap[:, 4:8], in_=SC[0].ap[:, 0:4], func=AF.Exp,
                bias=stat.ap[:, 1:2]), reads=[SC[0], stat], writes=[SC[0]])
            p.do("vector", lambda e: e.tensor_reduce(
                out=stat.ap[:, 2:3], in_=SC[0].ap[:, 4:8], axis=AX, op=ALU.add),
                reads=[SC[0]], writes=[stat])
            p.do("vector", lambda e: e.reciprocal(out=stat.ap[:, 3:4],
                                                  in_=stat.ap[:, 2:3]),
                 reads=[stat], writes=[stat])
            p.do("vector", lambda e: e.tensor_scalar(
                out=SC[0].ap[:, 8:12], in0=SC[0].ap[:, 4:8], scalar1=stat.ap[:, 3:4],
                scalar2=None, op0=ALU.mult), reads=[SC[0], stat], writes=[SC[0]])
            p.do("vector", lambda e: e.tensor_reduce(
                out=stat.ap[:, 4:5], in_=SC[0].ap[:, 8:12], axis=AX, op=ALU.max),
                reads=[SC[0]], writes=[stat])
            for c_ in range(4):
                p.do("vector", lambda e, c2=c_: e.tensor_scalar(
                    out=SC[0].ap[:, 12 + c2:13 + c2], in0=SC[0].ap[:, 8 + c2:9 + c2],
                    scalar1=stat.ap[:, 4:5], scalar2=None, op0=ALU.is_ge),
                    reads=[SC[0], stat], writes=[SC[0]])
            p.do("vector", lambda e: e.tensor_copy(
                out=SC[0].ap[:, 16:17], in_=SC[0].ap[:, 12:13]),
                reads=[SC[0]], writes=[SC[0]])
            p.do("vector", lambda e: e.tensor_scalar(
                out=SC[0].ap[:, 20:21], in0=SC[0].ap[:, 12:13], scalar1=-1.0,
                scalar2=1.0, op0=ALU.mult, op1=ALU.add), reads=[SC[0]], writes=[SC[0]])
            for c_ in range(1, 4):
                p.do("vector", lambda e, c2=c_: e.tensor_tensor(
                    out=SC[0].ap[:, 16 + c2:17 + c2], in0=SC[0].ap[:, 12 + c2:13 + c2],
                    in1=SC[0].ap[:, 20:21], op=ALU.mult), reads=[SC[0]], writes=[SC[0]])
                if c_ < 3:
                    p.do("vector", lambda e, c2=c_: e.tensor_scalar(
                        out=SC[0].ap[:, 21:22], in0=SC[0].ap[:, 12 + c2:13 + c2],
                        scalar1=-1.0, scalar2=1.0, op0=ALU.mult, op1=ALU.add),
                        reads=[SC[0]], writes=[SC[0]])
                    p.do("vector", lambda e: e.tensor_tensor(
                        out=SC[0].ap[:, 20:21], in0=SC[0].ap[:, 20:21],
                        in1=SC[0].ap[:, 21:22], op=ALU.mult),
                        reads=[SC[0]], writes=[SC[0]])
            for c_ in range(4):
                p.do("vector", lambda e, c2=c_, g2=qt * 4 + c_: e.tensor_tensor(
                    out=gates.ap[:, g2:g2 + 1], in0=SC[0].ap[:, 8 + c2:9 + c2],
                    in1=SC[0].ap[:, 16 + c2:17 + c2], op=ALU.mult),
                    reads=[SC[0]], writes=[gates])
            ps2 = PS[3]
            for c_ in range(NDC):
                mm3(ps2, ps2.ap[:, 0:15],
                    XTH[c_], XTH[c_].ap[:, qt * 128:(qt + 1) * 128],
                    XTL[c_], XTL[c_].ap[:, qt * 128:(qt + 1) * 128],
                    WDH, WDH.ap[:, c_ * 16:c_ * 16 + 15],
                    WDL, WDL.ap[:, c_ * 16:c_ * 16 + 15],
                    start=(c_ == 0), stop=(c_ == NDC - 1))
            p.do("scalar", lambda e, i=ps2.ap[:, 0:15]: e.activation(
                out=SC[1].ap[:, 0:15], in_=i, func=AF.Copy), reads=[ps2], writes=[SC[1]])
            off15 = 0
            for e_, r_ in enumerate((1, 2, 4, 8)):
                p.do("vector", lambda e, o15=off15, r2=r_, g2=qt * 4 + e_:
                     e.tensor_scalar(out=SC[1].ap[:, 16 + o15:16 + o15 + r2],
                                     in0=ones32.ap[:, 0:r2],
                                     scalar1=gates.ap[:, g2:g2 + 1], scalar2=None,
                                     op0=ALU.mult), reads=[ones32, gates],
                     writes=[SC[1]])
                off15 += r_
            p.do("vector", lambda e: e.tensor_tensor(
                out=SC[1].ap[:, 32:47], in0=SC[1].ap[:, 0:15], in1=SC[1].ap[:, 16:31],
                op=ALU.mult), reads=[SC[1]], writes=[SC[1]])
            split_hi_lo(SC[1], SC[1].ap[:, 32:47], EH[1], EH[1].ap[:, 0:15],
                        EL[1], EL[1].ap[:, 0:15])
            for half, (src16, dst16) in enumerate(((EH[1], GFH), (EL[1], GFL))):
                pt = PT[half]
                p.do("tensor", lambda e, i=src16.ap[:, 0:15], o=pt.ap[0:15, :]:
                     e.transpose(o, i, ident.ap[:, :]), reads=[src16, ident],
                     writes=[pt])
                p.do("vector", lambda e, i=pt.ap[0:15, :],
                     o=dst16.ap[0:15, qt * 128:(qt + 1) * 128]:
                     e.tensor_copy(out=o, in_=i), reads=[pt], writes=[dst16])
        p.barrier()
        c4b = _Carve(AR16, 24704)
        c4b.off = 2 * T
        WTH = c4b.take(4096, name="WTH"); WTL = c4b.take(4096, name="WTL")
        WUH = c4b.take(1024, rows=15, name="WUH"); WUL = c4b.take(1024, rows=15, name="WUL")
        p.dma_s(lambda e: e.dma_start(out=WS32.ap[0:15, 0:1024], in_=d_wut),
             writes=[WS32])
        split_hi_lo(WS32, WS32.ap[0:15, 0:1024], WUH, WUH.ap, WUL, WUL.ap)
        wslots = [Buf(WS32.ap[:, 0:512], "w4a"), Buf(WS32.ap[:, 512:1024], "w4b")]
        for og in range(2):
            for c_ in range(NDC):
                slb = wslots[c_ % 2]
                p.dma_s(lambda e, o=slb.ap, i=d_wt1[c_][:, og * 512:(og + 1) * 512]:
                     e.dma_start(out=o, in_=i), writes=[slb])
                split_hi_lo(slb, slb.ap, WTH, WTH.ap[:, c_ * 512:(c_ + 1) * 512],
                            WTL, WTL.ap[:, c_ * 512:(c_ + 1) * 512])
            for qt in range(NQT):
                ps = PS[qt % 2]
                for c_ in range(NDC):
                    mm3(ps, ps.ap[:, :],
                        XTH[c_], XTH[c_].ap[:, qt * 128:(qt + 1) * 128],
                        XTL[c_], XTL[c_].ap[:, qt * 128:(qt + 1) * 128],
                        WTH, WTH.ap[:, c_ * 512:(c_ + 1) * 512],
                        WTL, WTL.ap[:, c_ * 512:(c_ + 1) * 512],
                        start=(c_ == 0), stop=False)
                mm3(ps, ps.ap[:, :],
                    GFH, GFH.ap[0:15, qt * 128:(qt + 1) * 128],
                    GFL, GFL.ap[0:15, qt * 128:(qt + 1) * 128],
                    WUH, WUH.ap[0:15, og * 512:(og + 1) * 512],
                    WUL, WUL.ap[0:15, og * 512:(og + 1) * 512],
                    start=False, stop=True)
                p.do("vector", lambda e, i=ps.ap[:, :],
                     o=X[qt].ap[:, og * QC:(og + 1) * QC]: e.scalar_tensor_tensor(
                    out=o, in0=i, scalar=1.0, in1=o, op0=ALU.mult, op1=ALU.add),
                    reads=[ps, X[qt]], writes=[X[qt]])
        layernorm_inplace()

    l1_ffn()
    if debug_stop == "p4":
        store_out()
        with es, nc.Block() as block:
            p.emit(block)
        return nc
    store_out()
    with es, nc.Block() as block:
        p.emit(block)
    return nc

def _host_inputs(inputs):
    f32 = np.float32
    g = {k: np.asarray(v) for k, v in inputs.items()}
    out = {}
    out["ident"] = np.eye(128, dtype=ml_dtypes.bfloat16)
    k_idx = np.arange(128)[:, None]
    q_idx = np.arange(128)[None, :]
    out["cmask"] = np.where(k_idx > q_idx, f32(NEG_BIG), f32(0.0)).astype(f32)
    out["pos"] = np.arange(T, dtype=f32)[:, None].copy()
    out["jrow"] = np.arange(64, dtype=f32)[None, :].copy()
    wq = np.ascontiguousarray(g["L0_attn_W_Q"].reshape(1024, 1024).T)
    wk = np.ascontiguousarray(g["L0_attn_W_K"].reshape(1024, 1024).T)
    wv = np.ascontiguousarray(g["L0_attn_W_V"].reshape(1024, 1024).T)
    wqkv = np.empty((8, 8, 128, 384), dtype=f32)
    for hp in range(8):
        s = slice(hp * 128, (hp + 1) * 128)
        wqkv[hp] = np.concatenate([wq[:, s], wk[:, s], wv[:, s]], axis=1).reshape(8, 128, 384)
    out["l0_wqkv"] = wqkv
    bq = g["L0_attn_b_Q"].reshape(1024)
    bk = g["L0_attn_b_K"].reshape(1024)
    bv = g["L0_attn_b_V"].reshape(1024)
    bqkv = np.empty((8, 128, 3), dtype=f32)
    for hp in range(8):
        s = slice(hp * 128, (hp + 1) * 128)
        bqkv[hp] = np.stack([bq[s], bk[s], bv[s]], axis=1)
    out["l0_bqkv"] = bqkv
    wo = np.ascontiguousarray(g["L0_attn_W_O"].transpose(0, 2, 1).reshape(1024, 1024))
    out["l0_wo"] = np.ascontiguousarray(wo.reshape(8, 128, 1024))
    out["l0_bo"] = np.ascontiguousarray(g["L0_attn_b_O"].reshape(1, 1024))
    out["l0_w1t"] = np.ascontiguousarray(g["L0_ffn_w1"].T).reshape(8, 128, 4096)
    out["l0_b1"] = np.ascontiguousarray(g["L0_ffn_b1"].reshape(32, 128).T)
    out["l0_w2t"] = np.ascontiguousarray(g["L0_ffn_w2"].T).reshape(32, 128, 1024)
    out["l0_b2"] = np.ascontiguousarray(g["L0_ffn_b2"].reshape(1, 1024))
    w1q = g["L1_attn_wq"].reshape(1024, 1024)
    w1k = g["L1_attn_wk"].reshape(1024, 1024)
    w1v = g["L1_attn_wv"].reshape(1024, 1024)
    wqkv1 = np.empty((8, 8, 128, 384), dtype=f32)
    for h in range(8):
        s = slice(h * 128, (h + 1) * 128)
        wqkv1[h] = np.concatenate([w1q[:, s], w1k[:, s], w1v[:, s]],
                                  axis=1).reshape(8, 128, 384)
    out["l1_wqkv"] = wqkv1
    out["l1_wo"] = np.ascontiguousarray(np.asarray(g["L1_attn_wo"], dtype=f32))
    out["l1_wt"] = np.ascontiguousarray(g["L1_ffn_W"].T).reshape(8, 128, 1024)
    out["l1_wpt"] = np.ascontiguousarray(g["L1_ffn_Wp"].T).reshape(8, 128, 256)
    out["l1_bp"] = np.ascontiguousarray(g["L1_ffn_bp"].reshape(2, 128).T)
    out["l1_simt"] = np.ascontiguousarray(g["L1_ffn_sim"].T)
    out["l1_temp"] = np.ascontiguousarray(g["L1_ffn_temp"].reshape(1, 1))
    wd = np.concatenate([g[f"L1_ffn_Wd{i}"] for i in range(4)], axis=0)
    out["l1_wdt"] = np.ascontiguousarray(wd.T).reshape(8, 128, 15)
    wu = np.concatenate([g[f"L1_ffn_Wu{i}"] for i in range(4)], axis=1)
    out["l1_wut"] = np.ascontiguousarray(wu.T)
    return g, out


def kernel(**inputs):
    g, shared = _host_inputs(inputs)
    x = np.ascontiguousarray(np.asarray(g["L0_attn_x"], dtype=np.float32))
    nc = build()
    in_maps = []
    for b in range(8):
        m = {"x": x[b]}
        m.update(shared)
        in_maps.append(m)
    res = run_bass_kernel_spmd(nc, in_maps, core_ids=list(range(8)))
    return np.stack([res.results[b]["y"] for b in range(8)], axis=0)


# version 05
# speedup vs baseline: 1.0014x; 1.0014x over previous
import numpy as np
import ml_dtypes
from contextlib import ExitStack

import concourse.bass as bass
import concourse.mybir as mybir
from concourse.bass_utils import run_bass_kernel_spmd

F32 = mybir.dt.float32
BF16 = mybir.dt.bfloat16
ALU = mybir.AluOpType
AF = mybir.ActivationFunctionType
AX = mybir.AxisListType.X

T, D = 2048, 1024
NQT = T // 128
NDC = D // 128
QC = 512
NQC = T // QC

C23 = 12582912.0
C52 = 4503599627370496.0
EPS_DOM = 9.094947017729282e-13
NEG_BIG = -1.0e30
CLAMP_MAX = float(np.log(1.0 / 0.01))
LN1E4_64 = float(np.log(10000.0) / 64.0)
C_LN_HI = float(np.float32(LN1E4_64))
C_LN_LO = float(LN1E4_64 - C_LN_HI)
TWO_PI = 2.0 * float(np.pi)
C2PI_HI = float(np.float32(np.trunc(TWO_PI * 2048.0) / 2048.0))
C2PI_MID = float(np.float32(TWO_PI - C2PI_HI))
INV_2PI = float(np.float32(1.0 / TWO_PI))
HALF_PI = float(np.float32(np.pi / 2.0))


class Buf:
    __slots__ = ("ap", "wtok", "rtoks", "name")

    def __init__(self, ap, name=""):
        self.ap = ap
        self.name = name
        self.wtok = None
        self.rtoks = []


class Prog:
    ENGS = ("sync", "vector", "scalar", "tensor", "gpsimd")

    def __init__(self, nc, es):
        self.nc = nc
        self.ops = {e: [] for e in self.ENGS}
        self.cnt = {}
        self.sems = {}
        for s in ("sd", "gd", "ad", "v", "a", "t", "g", "sf"):
            self.sems[s] = es.enter_context(nc.semaphore(s))
            self.cnt[s] = 0

    def _sem_for(self, eng, dma):
        if eng == "sync":
            return ("sd", 16) if dma else ("sf", 1)
        if dma and eng == "gpsimd":
            return "gd", 16
        if dma and eng == "scalar":
            return "ad", 16
        return {"vector": "v", "scalar": "a", "tensor": "t", "gpsimd": "g"}[eng], 1

    def do(self, eng, fn, reads=(), writes=(), extra=(), dma=False):
        sem, amt = self._sem_for(eng, dma)
        waits = {}

        def add(tok):
            if tok is not None:
                s, v = tok
                waits[s] = max(waits.get(s, 0), v)
        for b in reads:
            add(b.wtok)
        for b in writes:
            add(b.wtok)
            for tok in b.rtoks:
                add(tok)
        for tok in extra:
            add(tok)
        self.cnt[sem] += amt
        tok = (sem, self.cnt[sem])
        self.ops[eng].append((waits, fn, sem, amt))
        for b in reads:
            b.rtoks.append(tok)
        for b in writes:
            b.wtok = tok
            b.rtoks = []
        return tok

    def dma_f(self, eng, fn, reads=(), writes=(), extra=()):
        ex = [t for t in extra if t is not None]
        for b in reads:
            if b.wtok is not None:
                ex.append(b.wtok)
        for b in writes:
            if b.wtok is not None:
                ex.append(b.wtok)
            ex.extend(b.rtoks)
        t = self.do(eng, fn, extra=ex, dma=True)
        tg = self.do(eng, lambda e: e.nop(), extra=[t])
        for b in reads:
            b.rtoks.append(tg)
        for b in writes:
            b.wtok = tg
            b.rtoks = []
        return tg

    def dma_s(self, fn, reads=(), writes=(), extra=()):
        return self.dma_f("sync", fn, reads, writes, extra)

    def dma_a(self, fn, reads=(), writes=(), extra=()):
        return self.dma_f("scalar", fn, reads, writes, extra)

    def dma_g(self, fn, reads=(), writes=(), extra=()):
        ex = [t for t in extra if t is not None]
        for b in reads:
            if b.wtok is not None:
                ex.append(b.wtok)
        for b in writes:
            if b.wtok is not None:
                ex.append(b.wtok)
            ex.extend(b.rtoks)
        t = self.do("gpsimd", fn, extra=ex, dma=True)
        tg = self.do("gpsimd", lambda e: e.nop(), extra=[t])
        for b in reads:
            b.rtoks.append(tg)
        for b in writes:
            b.wtok = tg
            b.rtoks = []
        return tg

    def flush_g(self, bufs):
        ex = []
        for b in bufs:
            if b.wtok is not None:
                ex.append(b.wtok)
            ex.extend(b.rtoks)
        tg = self.do("gpsimd", lambda e: e.nop(), extra=ex)
        for b in bufs:
            b.wtok = tg
            b.rtoks = []
        return tg

    def barrier(self):
        toks = [(s, self.cnt[s]) for s in self.cnt]
        for eng in self.ENGS:
            self.do(eng, lambda e: e.nop(), extra=toks)

    def emit(self, block):
        waited = {e: {} for e in self.ENGS}

        def runner(eng_name):
            def _(eng):
                w = waited[eng_name]
                for waits, fn, sem, amt in self.ops[eng_name]:
                    for s in sorted(waits):
                        v = waits[s]
                        if w.get(s, 0) < v:
                            eng.wait_ge(self.sems[s], v)
                            w[s] = v
                    inst = fn(eng)
                    inst.then_inc(self.sems[sem], amt)
            return _
        block.sync(runner("sync"))
        block.vector(runner("vector"))
        block.scalar(runner("scalar"))
        block.tensor(runner("tensor"))
        block.gpsimd(runner("gpsimd"))


def build(debug_stop=None):
    nc = bass.Bass("TRN2", target_bir_lowering=False)
    es = ExitStack()

    dx = nc.dram_tensor("x", [T, D], F32, kind="ExternalInput").ap()
    dy = nc.dram_tensor("y", [T, D], F32, kind="ExternalOutput").ap()
    d_ident = nc.dram_tensor("ident", [128, 128], BF16, kind="ExternalInput").ap()
    d_cmask = nc.dram_tensor("cmask", [128, 128], F32, kind="ExternalInput").ap()
    d_pos = nc.dram_tensor("pos", [T, 1], F32, kind="ExternalInput").ap()
    d_jrow = nc.dram_tensor("jrow", [1, 64], F32, kind="ExternalInput").ap()
    d_wqkv0 = nc.dram_tensor("l0_wqkv", [8, 8, 128, 384], F32, kind="ExternalInput").ap()
    d_bqkv0 = nc.dram_tensor("l0_bqkv", [8, 128, 3], F32, kind="ExternalInput").ap()
    d_wo0 = nc.dram_tensor("l0_wo", [8, 128, 1024], F32, kind="ExternalInput").ap()
    d_bo0 = nc.dram_tensor("l0_bo", [1, 1024], F32, kind="ExternalInput").ap()
    d_w1t = nc.dram_tensor("l0_w1t", [8, 128, 4096], F32, kind="ExternalInput").ap()
    d_b1 = nc.dram_tensor("l0_b1", [128, 32], F32, kind="ExternalInput").ap()
    d_w2t = nc.dram_tensor("l0_w2t", [32, 128, 1024], F32, kind="ExternalInput").ap()
    d_b2 = nc.dram_tensor("l0_b2", [1, 1024], F32, kind="ExternalInput").ap()
    d_wqkv1 = nc.dram_tensor("l1_wqkv", [8, 8, 128, 384], F32, kind="ExternalInput").ap()
    d_wo1 = nc.dram_tensor("l1_wo", [8, 128, 1024], F32, kind="ExternalInput").ap()
    d_wt1 = nc.dram_tensor("l1_wt", [8, 128, 1024], F32, kind="ExternalInput").ap()
    d_wpt = nc.dram_tensor("l1_wpt", [8, 128, 256], F32, kind="ExternalInput").ap()
    d_bp = nc.dram_tensor("l1_bp", [128, 2], F32, kind="ExternalInput").ap()
    d_simt = nc.dram_tensor("l1_simt", [4, 256], F32, kind="ExternalInput").ap()
    d_temp = nc.dram_tensor("l1_temp", [1, 1], F32, kind="ExternalInput").ap()
    d_wdt = nc.dram_tensor("l1_wdt", [8, 128, 15], F32, kind="ExternalInput").ap()
    d_wut = nc.dram_tensor("l1_wut", [15, 1024], F32, kind="ExternalInput").ap()

    d_hhi = nc.dram_tensor("s_hhi", [4096, T], BF16).ap()
    d_hlo = nc.dram_tensor("s_hlo", [4096, T], BF16).ap()
    d_w1q = nc.dram_tensor("s_w1q", [8, 128, 4096], BF16).ap()
    d_w2q = nc.dram_tensor("s_w2q", [32, 128, 1024], BF16).ap()
    d_hq = nc.dram_tensor("s_hq", [32, 128, T], BF16).ap()
    d_sc = nc.dram_tensor("s_sc", [128, 8], F32).ap()
    d_sc2 = nc.dram_tensor("s_sc2", [1, 8], F32).ap()
    d_ccin = nc.dram_tensor("s_ccin", [1, 8], F32).ap()
    d_ccout = nc.dram_tensor("s_ccout", [1, 8], F32).ap()

    def sb(name, shape, dt=F32):
        return Buf(es.enter_context(nc.sbuf_tensor(name, shape, dt)), name)

    X = [sb(f"X{i}", [128, D]) for i in range(NQT)]
    ident = sb("identt", [128, 128], BF16)
    cmask = sb("cmaskt", [128, 128])
    ident32 = sb("ident32", [128, 128])
    onescol16 = sb("ones16c", [128, 1], BF16)
    ones32 = sb("ones32", [128, 16])
    scal = sb("scal", [128, 16])
    stat = sb("stat", [128, 8])
    const32 = sb("const32", [128, 4])
    gates = sb("gates", [128, 64])
    qsc = sb("qsc", [128, 12])
    XTH = [sb(f"XTH{c}", [128, T], BF16) for c in range(NDC)]
    XTL = [sb(f"XTL{c}", [128, T], BF16) for c in range(NDC)]
    WS32 = sb("WS32", [128, 2048])
    AR16 = sb("AR16", [128, 24704], BF16)
    AR32 = sb("AR32", [128, 4608])

    delta = Buf(scal.ap[:, 12:13], "delta")
    negM = Buf(scal.ap[:, 13:14], "negM")
    bcol = Buf(scal.ap[:, 8:12], "bcol")

    class _Carve:
        def __init__(self, arena, total):
            self.arena = arena
            self.off = 0
            self.total = total

        def take(self, n, rows=128, name=""):
            a = self.arena.ap[0:rows, self.off:self.off + n]
            self.off += n
            assert self.off <= self.total, (name, self.off)
            return Buf(a, name)

    c16 = _Carve(AR16, 24704)
    QH = c16.take(T, name="QH"); QL = c16.take(T, name="QL")
    KH = c16.take(T, name="KH"); KL = c16.take(T, name="KL")
    _vh = c16.take(NQT * 128); _vl = c16.take(NQT * 128)
    VH = [Buf(_vh.ap[:, k * 128:(k + 1) * 128], f"VH{k}") for k in range(NQT)]
    VL = [Buf(_vl.ap[:, k * 128:(k + 1) * 128], f"VL{k}") for k in range(NQT)]
    AH = c16.take(QC, name="AH"); AL = c16.take(QC, name="AL")
    WAH = c16.take(3072, name="WAH"); WAL = c16.take(3072, name="WAL")
    WBH = c16.take(1024, name="WBH"); WBL = c16.take(1024, name="WBL")
    _e16 = c16.take(4 * QC)
    EH = [Buf(_e16.ap[:, i * QC:(i + 1) * QC], f"EH{i}") for i in range(2)]
    EL = [Buf(_e16.ap[:, (2 + i) * QC:(3 + i) * QC], f"EL{i}") for i in range(2)]
    onesrow16b = c16.take(128, rows=33, name="ones16rb")
    _drh = c16.take(QC, rows=33); _drl = c16.take(QC, rows=33)
    drh_b, drl_b = _drh, _drl

    c32 = _Carve(AR32, 4608)
    SC = [c32.take(QC, name=f"SC{i}") for i in range(3)]
    brow = c32.take(1024, name="brow")
    bc32 = c32.take(QC, name="bc32")
    _drow = c32.take(QC, rows=33)
    drow_b = _drow
    _sinbase = c32.take(1024, name="sinb")
    COS = [Buf(brow.ap[:, (i % 16) * 64:(i % 16) * 64 + 64], f"COS{i}") for i in range(NQT)]
    SIN = [Buf(_sinbase.ap[:, (i % 16) * 64:(i % 16) * 64 + 64], f"SIN{i}") for i in range(NQT)]
    h32 = Buf(AR32.ap[:, 2560:4608], "h32")

    PS = [Buf(es.enter_context(nc.psum_tensor(f"PS{i}", [128, QC], F32)), f"PS{i}")
          for i in range(4)]
    PSD = Buf(es.enter_context(nc.psum_tensor("PSD", [64, QC], F32)), "PSD")
    PT = [Buf(es.enter_context(nc.psum_tensor(f"PTx{i}", [128, 128], BF16)), f"PT{i}")
          for i in range(2)]
    PT2 = Buf(es.enter_context(nc.psum_tensor("PT2", [128, 128], F32)), "PT2")

    p = Prog(nc, es)

    def split_hi_lo(src, src_ap, hi, hi_ap, lo, lo_ap):
        p.do("vector", lambda e, i=src_ap, o=hi_ap: e.tensor_copy(out=o, in_=i),
             reads=[src], writes=[hi])
        p.do("vector", lambda e, i0=hi_ap, i1=src_ap, o=lo_ap: e.scalar_tensor_tensor(
            out=o, in0=i0, scalar=-1.0, in1=i1, op0=ALU.mult, op1=ALU.add),
            reads=[hi, src], writes=[lo])

    def mm3(ps, ps_ap, lh, lh_ap, ll, ll_ap, rh, rh_ap, rl, rl_ap, start, stop, tp=None):
        p.do("tensor", lambda e, o=ps_ap, a=lh_ap, b=rh_ap, s=start, t_=tp: e.matmul(
            o, a, b, start=s, stop=False, tile_position=t_), reads=[lh, rh], writes=[ps])
        p.do("tensor", lambda e, o=ps_ap, a=lh_ap, b=rl_ap, t_=tp: e.matmul(
            o, a, b, start=False, stop=False, tile_position=t_), reads=[lh, rl], writes=[ps])
        p.do("tensor", lambda e, o=ps_ap, a=ll_ap, b=rh_ap, s=stop, t_=tp: e.matmul(
            o, a, b, start=False, stop=s, tile_position=t_), reads=[ll, rh], writes=[ps])

    def bcast_scalar(src, src_ap, dst, dst_ap, slot):
        t1 = p.dma_g(lambda e, i=src_ap, o=d_sc2[0:1, slot:slot + 1]:
                     e.dma_start(out=o, in_=i), reads=[src])
        p.dma_g(lambda e, o=dst_ap,
                i=d_sc2[0:1, slot:slot + 1].partition_broadcast(128):
                e.dma_start(out=o, in_=i), writes=[dst], extra=[t1])

    def build_xt():
        for c in range(NDC):
            for qt in range(NQT):
                p.do("tensor", lambda e, i=X[qt].ap[:, c * 128:(c + 1) * 128], o=PT2.ap[:, :]:
                     e.transpose(o, i, ident32.ap[:, :]), reads=[X[qt], ident32], writes=[PT2])
                p.do("vector", lambda e, i=PT2.ap[:, :], o=XTH[c].ap[:, qt * 128:(qt + 1) * 128]:
                     e.tensor_copy(out=o, in_=i), reads=[PT2], writes=[XTH[c]])
                p.do("vector", lambda e, i0=XTH[c].ap[:, qt * 128:(qt + 1) * 128],
                     i1=PT2.ap[:, :], o=XTL[c].ap[:, qt * 128:(qt + 1) * 128]:
                     e.scalar_tensor_tensor(out=o, in0=i0, scalar=-1.0, in1=i1,
                                            op0=ALU.mult, op1=ALU.add),
                     reads=[XTH[c], PT2], writes=[XTL[c]])

    def layernorm_inplace():
        for qt in range(NQT):
            xb = X[qt]
            p.do("vector", lambda e, i=xb.ap[:, :]: e.tensor_reduce(
                out=stat.ap[:, 0:1], in_=i, axis=AX, op=ALU.add), reads=[xb], writes=[stat])
            p.do("vector", lambda e: e.tensor_scalar(
                out=stat.ap[:, 1:2], in0=stat.ap[:, 0:1], scalar1=1.0 / D, scalar2=None,
                op0=ALU.mult), reads=[stat], writes=[stat])
            p.do("scalar", lambda e, i=xb.ap[:, 0:QC]: e.activation(
                out=SC[0].ap, in_=i, func=AF.Square, accum_out=stat.ap[:, 2:3]),
                reads=[xb], writes=[SC[0], stat])
            p.do("scalar", lambda e, i=xb.ap[:, QC:D]: e.activation(
                out=SC[1].ap, in_=i, func=AF.Square, accum_out=stat.ap[:, 3:4]),
                reads=[xb], writes=[SC[1], stat])
            p.do("vector", lambda e: e.tensor_tensor(
                out=stat.ap[:, 2:3], in0=stat.ap[:, 2:3], in1=stat.ap[:, 3:4], op=ALU.add),
                reads=[stat], writes=[stat])
            p.do("vector", lambda e: e.tensor_tensor(
                out=stat.ap[:, 4:5], in0=stat.ap[:, 1:2], in1=stat.ap[:, 1:2], op=ALU.mult),
                reads=[stat], writes=[stat])
            p.do("vector", lambda e: e.tensor_scalar(
                out=stat.ap[:, 5:6], in0=stat.ap[:, 2:3], scalar1=1.0 / D,
                scalar2=stat.ap[:, 4:5], op0=ALU.mult, op1=ALU.subtract),
                reads=[stat], writes=[stat])
            p.do("scalar", lambda e: e.activation(
                out=stat.ap[:, 6:7], in_=stat.ap[:, 5:6], func=AF.Sqrt, bias=1e-5),
                reads=[stat, const32], writes=[stat])
            p.do("vector", lambda e: e.reciprocal(out=stat.ap[:, 7:8], in_=stat.ap[:, 6:7]),
                 reads=[stat], writes=[stat])
            p.do("vector", lambda e, i=xb.ap[:, :]: e.tensor_scalar(
                out=i, in0=i, scalar1=stat.ap[:, 1:2], scalar2=stat.ap[:, 7:8],
                op0=ALU.subtract, op1=ALU.mult), reads=[xb, stat], writes=[xb])

    def store_out():
        toks = []
        for qt in range(NQT):
            toks.append(p.do("scalar", lambda e, i=X[qt].ap[:, :],
                             o=dy[qt * 128:(qt + 1) * 128, :]:
                             e.dma_start(out=o, in_=i), reads=[X[qt]], dma=True))
        p.do("scalar", lambda e: e.nop(), extra=[toks[-1]])

    for qt in range(NQT):
        p.dma_a(lambda e, o=X[qt].ap[:, :], i=dx[qt * 128:(qt + 1) * 128, :]:
                e.dma_start(out=o, in_=i), writes=[X[qt]])
    p.dma_s(lambda e: e.dma_start(out=ident.ap[:, :], in_=d_ident),
         writes=[ident])
    p.dma_s(lambda e: e.dma_start(out=cmask.ap[:, :], in_=d_cmask),
         writes=[cmask])
    p.do("vector", lambda e: e.tensor_copy(out=ident32.ap[:, :], in_=ident.ap[:, :]),
         reads=[ident], writes=[ident32])
    p.do("vector", lambda e: e.memset(ones32.ap[:, :], 1.0), writes=[ones32])
    p.do("vector", lambda e: e.memset(const32.ap[:, 0:1], 1e-5), writes=[const32])
    p.do("vector", lambda e: e.memset(const32.ap[:, 1:2], 1e-6), writes=[const32])
    p.do("vector", lambda e: e.memset(const32.ap[:, 2:3], 0.125), writes=[const32])
    p.do("vector", lambda e: e.memset(const32.ap[:, 3:4], 0.0078125), writes=[const32])
    nc.const_aps.register(F32, 1e-5, const32.ap[:, 0:1])
    nc.const_aps.register(F32, 1e-6, const32.ap[:, 1:2])
    nc.const_aps.register(F32, 0.125, const32.ap[:, 2:3])
    nc.const_aps.register(F32, 0.0078125, const32.ap[:, 3:4])
    p.do("vector", lambda e: e.memset(onesrow16b.ap[:, :], 1.0), writes=[onesrow16b])
    p.do("vector", lambda e: e.memset(onescol16.ap[:, :], 1.0), writes=[onescol16])
    p.do("vector", lambda e: e.tensor_scalar(
        out=scal.ap[:, 0:1], in0=ones32.ap[:, 0:1], scalar1=EPS_DOM, scalar2=None,
        op0=ALU.add), reads=[ones32], writes=[scal])
    p.do("vector", lambda e: e.tensor_scalar(
        out=delta.ap, in0=scal.ap[:, 0:1], scalar1=1.0, scalar2=None,
        op0=ALU.is_le), reads=[scal], writes=[delta])

    def attn_inner(causal, layer, dbg=False):
        nheads = 2 if layer == 0 else 1
        dh = 64 if layer == 0 else 128
        for qc in range(NQC):
            ktmax = NQT if not causal else min(NQT, (qc + 1) * (QC // 128))
            for kt in range(ktmax):
                qlo = qc * QC
                off = kt * 128 - qlo if (causal and kt * 128 > qlo) else 0
                for h in range(nheads):
                    base = h * 64
                    pss = PS[h]
                    sl = slice(base, base + dh)
                    mm3(pss, pss.ap[:, off:QC],
                        KH, KH.ap[sl, kt * 128:(kt + 1) * 128],
                        KL, KL.ap[sl, kt * 128:(kt + 1) * 128],
                        QH, QH.ap[sl, qlo + off:qlo + QC],
                        QL, QL.ap[sl, qlo + off:qlo + QC],
                        start=True, stop=True)
                    e32 = SC[h]
                    if causal and qlo <= kt * 128 < qlo + QC:
                        p.do("vector", lambda e, i=pss.ap[:, off:off + 128],
                             o=SC[1].ap[:, 0:128]: e.tensor_tensor(
                            out=o, in0=i, in1=cmask.ap[:, :], op=ALU.add),
                            reads=[pss, cmask], writes=[SC[1]])
                        p.do("scalar", lambda e, i=SC[1].ap[:, 0:128],
                             o=e32.ap[:, off:off + 128]: e.activation(
                            out=o, in_=i, func=AF.Exp, scale=0.0078125),
                            reads=[SC[1], const32], writes=[e32])
                        if QC - off - 128 > 0:
                            p.do("scalar", lambda e, i=pss.ap[:, off + 128:QC],
                                 o=e32.ap[:, off + 128:QC]: e.activation(
                                out=o, in_=i, func=AF.Exp, scale=0.0078125),
                                reads=[pss, const32], writes=[e32])
                    else:
                        if layer == 0:
                            p.do("scalar", lambda e, i=pss.ap[:, off:QC], o=e32.ap[:, off:QC]:
                                 e.activation(out=o, in_=i, func=AF.Exp, bias=negM.ap,
                                              scale=0.125),
                                 reads=[pss, negM, const32], writes=[e32])
                        else:
                            p.do("scalar", lambda e, i=pss.ap[:, off:QC], o=e32.ap[:, off:QC]:
                                 e.activation(out=o, in_=i, func=AF.Exp, scale=0.0078125),
                                 reads=[pss, const32], writes=[e32])
                    eh, el = EH[h], EL[h]
                    split_hi_lo(e32, e32.ap[:, off:QC], eh, eh.ap[:, off:QC],
                                el, el.ap[:, off:QC])
                    di = h * 32
                    dtp = None if h == 0 else (0, 32)
                    p.do("tensor", lambda e, o=PSD.ap[di:di + 1, off:QC],
                         a=onescol16.ap[:, 0:1], b=eh.ap[:, off:QC], s=(kt == 0), t_=dtp:
                         e.matmul(o, a, b, start=s, stop=False, tile_position=t_,
                                  skip_group_check=True),
                         reads=[onescol16, eh], writes=[PSD])
                    p.do("tensor", lambda e, o=PSD.ap[di:di + 1, off:QC],
                         a=onescol16.ap[:, 0:1], b=el.ap[:, off:QC], s=(kt == ktmax - 1),
                         t_=dtp: e.matmul(o, a, b, start=False, stop=s, tile_position=t_,
                                          skip_group_check=True),
                         reads=[onescol16, el], writes=[PSD])
                    vsl = slice(h * 64, h * 64 + dh) if layer == 0 else slice(0, 128)
                    tp = None if (layer == 1 or h == 0) else (0, 64)
                    outsl = PS[2].ap[h * 64:h * 64 + dh, off:QC] if layer == 0 \
                        else PS[2].ap[:, off:QC]
                    p.do("tensor", lambda e, o=outsl, a=VH[kt].ap[:, vsl],
                         b=eh.ap[:, off:QC], s=(kt == 0), t_=tp:
                         e.matmul(o, a, b, start=s, stop=False, tile_position=t_,
                                  skip_group_check=True),
                         reads=[VH[kt], eh], writes=[PS[2]])
                    p.do("tensor", lambda e, o=outsl, a=VH[kt].ap[:, vsl],
                         b=el.ap[:, off:QC], t_=tp:
                         e.matmul(o, a, b, start=False, stop=False, tile_position=t_,
                                  skip_group_check=True),
                         reads=[VH[kt], el], writes=[PS[2]])
                    p.do("tensor", lambda e, o=outsl, a=VL[kt].ap[:, vsl],
                         b=eh.ap[:, off:QC], s=(kt == ktmax - 1), t_=tp:
                         e.matmul(o, a, b, start=False, stop=s, tile_position=t_,
                                  skip_group_check=True),
                         reads=[VL[kt], eh], writes=[PS[2]])
            for h in range(nheads):
                di = h * 32
                drsl = slice(0, 1) if h == 0 else slice(32, 33)
                p.do("vector", lambda e, i=PSD.ap[di:di + 1, :], o=drow_b.ap[drsl, :]:
                     e.reciprocal(out=o, in_=i), reads=[PSD], writes=[drow_b])
                split_hi_lo(drow_b, drow_b.ap[drsl, :], drh_b, drh_b.ap[drsl, :],
                            drl_b, drl_b.ap[drsl, :])
                osl = PS[3].ap[h * 64:h * 64 + 64, :] if layer == 0 else PS[3].ap[:, :]
                tp = None if h == 0 else (32, 64)
                m = 64 if layer == 0 else 128
                ogr = onesrow16b.ap[0:1, 0:m] if h == 0 else onesrow16b.ap[32:33, 0:m]
                p.do("tensor", lambda e, o=osl, a=ogr, b=drh_b.ap[drsl, :], t_=tp:
                     e.matmul(o, a, b, start=True, stop=False, tile_position=t_,
                              skip_group_check=True),
                     reads=[onesrow16b, drh_b], writes=[PS[3]])
                p.do("tensor", lambda e, o=osl, a=ogr, b=drl_b.ap[drsl, :], t_=tp:
                     e.matmul(o, a, b, start=False, stop=True, tile_position=t_,
                              skip_group_check=True),
                     reads=[onesrow16b, drl_b], writes=[PS[3]])
            p.do("scalar", lambda e: e.activation(out=bc32.ap, in_=PS[3].ap[:, :],
                                                  func=AF.Copy),
                 reads=[PS[3]], writes=[bc32])
            p.do("vector", lambda e: e.tensor_tensor(
                out=SC[2].ap, in0=PS[2].ap[:, :], in1=bc32.ap, op=ALU.mult),
                reads=[PS[2], bc32], writes=[SC[2]])
            split_hi_lo(SC[2], SC[2].ap, AH, AH.ap, AL, AL.ap)
            if dbg:
                p.do("vector", lambda e, o=X[qc].ap[:, 0:QC], i=SC[2].ap:
                     e.tensor_copy(out=o, in_=i), reads=[SC[2]], writes=[X[qc]])
                p.do("vector", lambda e, o=X[4 + qc].ap[:, 0:QC], i=bc32.ap:
                     e.tensor_copy(out=o, in_=i), reads=[bc32], writes=[X[4 + qc]])
                continue
            for j in range(QC // 128):
                qt = qc * (QC // 128) + j
                for og in range(2):
                    ps = PS[og]
                    mm3(ps, ps.ap[:, :],
                        AH, AH.ap[:, j * 128:(j + 1) * 128],
                        AL, AL.ap[:, j * 128:(j + 1) * 128],
                        WBH, WBH.ap[:, og * QC:(og + 1) * QC],
                        WBL, WBL.ap[:, og * QC:(og + 1) * QC],
                        start=True, stop=True)
                    p.do("vector", lambda e, i=ps.ap[:, :],
                         o=X[qt].ap[:, og * QC:(og + 1) * QC]: e.scalar_tensor_tensor(
                        out=o, in0=i, scalar=1.0, in1=o, op0=ALU.mult, op1=ALU.add),
                        reads=[ps, X[qt]], writes=[X[qt]])

    def l0_attention():
        build_xt()
        wslots = [Buf(WS32.ap[:, 0:384], "ws0"), Buf(WS32.ap[:, 384:768], "ws1")]
        woslots = [Buf(WS32.ap[:, 768:1280], "wo0"), Buf(WS32.ap[:, 1280:1792], "wo1")]
        for hp in range(8):
            p.dma_s(lambda e, o=bcol.ap[:, 0:3], i=d_bqkv0[hp]:
                 e.dma_start(out=o, in_=i), writes=[bcol])
            for c in range(NDC):
                slb = wslots[c % 2]
                p.dma_s(lambda e, o=slb.ap, i=d_wqkv0[hp, c]:
                     e.dma_start(out=o, in_=i), writes=[slb])
                split_hi_lo(slb, slb.ap, WAH, WAH.ap[:, c * 384:(c + 1) * 384],
                            WAL, WAL.ap[:, c * 384:(c + 1) * 384])
            for half in range(2):
                slb = woslots[half]
                p.dma_s(lambda e, o=slb.ap, i=d_wo0[hp][:, half * 512:(half + 1) * 512]:
                     e.dma_start(out=o, in_=i), writes=[slb])
                split_hi_lo(slb, slb.ap, WBH, WBH.ap[:, half * 512:(half + 1) * 512],
                            WBL, WBL.ap[:, half * 512:(half + 1) * 512])
            for pi, (dsth, dstl) in enumerate(((QH, QL), (KH, KL))):
                for tc in range(NQC):
                    ps = PS[pi]
                    for c in range(NDC):
                        mm3(ps, ps.ap[:, :],
                            WAH, WAH.ap[:, c * 384 + pi * 128:c * 384 + (pi + 1) * 128],
                            WAL, WAL.ap[:, c * 384 + pi * 128:c * 384 + (pi + 1) * 128],
                            XTH[c], XTH[c].ap[:, tc * QC:(tc + 1) * QC],
                            XTL[c], XTL[c].ap[:, tc * QC:(tc + 1) * QC],
                            start=(c == 0), stop=(c == NDC - 1))
                    e32 = SC[2]
                    p.do("scalar", lambda e, i=ps.ap[:, :], o=e32.ap, b=bcol.ap[:, pi:pi + 1]:
                         e.activation(out=o, in_=i, func=AF.Identity, bias=b),
                         reads=[ps, bcol], writes=[e32])
                    p.do("vector", lambda e, i=e32.ap: e.tensor_reduce(
                        out=stat.ap[:, 0:1], in_=i, axis=AX, op=ALU.max,
                        apply_absolute_value=True), reads=[e32], writes=[stat])
                    if tc == 0:
                        p.do("vector", lambda e, s2=2 + pi: e.tensor_copy(
                            out=scal.ap[:, s2:s2 + 1], in_=stat.ap[:, 0:1]),
                            reads=[stat], writes=[scal])
                    else:
                        p.do("vector", lambda e, s2=2 + pi: e.tensor_tensor(
                            out=scal.ap[:, s2:s2 + 1], in0=scal.ap[:, s2:s2 + 1],
                            in1=stat.ap[:, 0:1], op=ALU.max), reads=[stat, scal],
                            writes=[scal])
                    split_hi_lo(e32, e32.ap, dsth, dsth.ap[:, tc * QC:(tc + 1) * QC],
                                dstl, dstl.ap[:, tc * QC:(tc + 1) * QC])
            t1 = p.dma_g(lambda e: e.dma_start(out=d_sc[:, 0:2], in_=scal.ap[:, 2:4]),
                         reads=[scal])
            p.dma_g(lambda e: e.dma_start(
                out=SC[1].ap[0:1, 0:128], in_=d_sc[:, 0:1].transpose([1, 0]),
                allow_slow_non_contiguous=True), writes=[SC[1]], extra=[t1])
            p.dma_g(lambda e: e.dma_start(
                out=SC[1].ap[0:1, 128:256], in_=d_sc[:, 1:2].transpose([1, 0]),
                allow_slow_non_contiguous=True), writes=[SC[1]], extra=[t1])
            p.do("vector", lambda e: e.tensor_reduce(
                out=scal.ap[0:1, 5:6], in_=SC[1].ap[0:1, 0:128], axis=AX, op=ALU.max),
                reads=[SC[1]], writes=[scal])
            p.do("vector", lambda e: e.tensor_reduce(
                out=scal.ap[0:1, 6:7], in_=SC[1].ap[0:1, 128:256], axis=AX, op=ALU.max),
                reads=[SC[1]], writes=[scal])
            p.do("vector", lambda e: e.tensor_tensor(
                out=scal.ap[0:1, 4:5], in0=scal.ap[0:1, 5:6], in1=scal.ap[0:1, 6:7],
                op=ALU.mult), reads=[scal], writes=[scal])
            p.do("vector", lambda e: e.tensor_scalar(
                out=scal.ap[0:1, 4:5], in0=scal.ap[0:1, 4:5], scalar1=-8.0, scalar2=None,
                op0=ALU.mult), reads=[scal], writes=[scal])
            bcast_scalar(scal, scal.ap[0:1, 4:5], negM, negM.ap, 0)
            for tc in range(NQC):
                ps = PS[2]
                for c in range(NDC):
                    mm3(ps, ps.ap[:, :],
                        WAH, WAH.ap[:, c * 384 + 256:c * 384 + 384],
                        WAL, WAL.ap[:, c * 384 + 256:c * 384 + 384],
                        XTH[c], XTH[c].ap[:, tc * QC:(tc + 1) * QC],
                        XTL[c], XTL[c].ap[:, tc * QC:(tc + 1) * QC],
                        start=(c == 0), stop=(c == NDC - 1))
                e32 = SC[2]
                p.do("scalar", lambda e, i=ps.ap[:, :], o=e32.ap, b=bcol.ap[:, 2:3]:
                     e.activation(out=o, in_=i, func=AF.Identity, bias=b),
                     reads=[ps, bcol], writes=[e32])
                split_hi_lo(e32, e32.ap, EH[0], EH[0].ap[:, :], EL[0], EL[0].ap[:, :])
                for j in range(QC // 128):
                    kt = tc * 4 + j
                    for src, dst in ((EH[0], VH[kt]), (EL[0], VL[kt])):
                        pt = PT[j % 2]
                        p.do("tensor", lambda e, i=src.ap[:, j * 128:(j + 1) * 128],
                             o=pt.ap[:, :]: e.transpose(o, i, ident.ap[:, :]),
                             reads=[src, ident], writes=[pt])
                        p.do("vector", lambda e, i=pt.ap[:, :], o=dst.ap:
                             e.tensor_copy(out=o, in_=i), reads=[pt], writes=[dst])
            attn_inner(causal=False, layer=0)
        p.dma_g(lambda e: e.dma_start(out=brow.ap,
                                      in_=d_bo0.partition_broadcast(128)),
                writes=[brow])
        for qt in range(NQT):
            p.do("vector", lambda e, i=X[qt].ap[:, :]: e.tensor_tensor(
                out=i, in0=i, in1=brow.ap, op=ALU.add), reads=[X[qt], brow], writes=[X[qt]])
        layernorm_inplace()

    l0_attention()
    if debug_stop == "p1":
        store_out()
        with es, nc.Block() as block:
            p.emit(block)
        return nc

    dcc_in = Buf(d_ccin, "dcc_in")
    dcc_out = Buf(d_ccout, "dcc_out")
    dsc = Buf(d_sc, "dsc")
    b_w1q = Buf(d_w1q, "b_w1q")
    b_w2q = Buf(d_w2q, "b_w2q")
    b_hq = Buf(d_hq, "b_hq")
    b_hhi = Buf(d_hhi, "b_hhi")
    b_hlo = Buf(d_hlo, "b_hlo")

    def pmax_roundtrip(src_col_ap, src_buf, dst_slot):
        t1 = p.dma_g(lambda e, i=src_col_ap: e.dma_start(
            out=d_sc[:, 4:5], in_=i, allow_slow_non_contiguous=True),
            reads=[src_buf], writes=[dsc])
        p.dma_g(lambda e: e.dma_start(
            out=SC[1].ap[0:1, 0:128], in_=d_sc[:, 4:5].transpose([1, 0]),
            allow_slow_non_contiguous=True), reads=[dsc], writes=[SC[1]], extra=[t1])
        p.do("vector", lambda e, sl2=dst_slot: e.tensor_reduce(
            out=qsc.ap[0:1, sl2:sl2 + 1], in_=SC[1].ap[0:1, 0:128], axis=AX, op=ALU.max),
            reads=[SC[1]], writes=[qsc])

    def allreduce_max_slot(slot):
        t0 = p.do("vector", lambda e, sl2=slot: e.tensor_scalar(
            out=SC[1].ap[0:1, 0:8], in0=ones32.ap[0:1, 0:8],
            scalar1=qsc.ap[0:1, sl2:sl2 + 1], scalar2=None, op0=ALU.mult),
            reads=[ones32, qsc], writes=[SC[1]])
        t1 = p.dma_g(lambda e: e.dma_start(
            out=d_ccin[:, :], in_=SC[1].ap[0:1, 0:8]),
            reads=[SC[1]], writes=[dcc_in])
        t2 = p.do("gpsimd", lambda e: e.collective_compute(
            "AllReduce", ALU.max, replica_groups=[list(range(8))],
            ins=[d_ccin], outs=[d_ccout]), reads=[dcc_in], writes=[dcc_out], extra=[t1])
        p.dma_g(lambda e, sl2=slot: e.dma_start(
            out=qsc.ap[:, sl2:sl2 + 1],
            in_=d_ccout[0:1, 0:1].partition_broadcast(128)),
            reads=[dcc_out], writes=[qsc])

    def bcast_slot(slot):
        t1 = p.dma_g(lambda e, sl2=slot: e.dma_start(
            out=d_sc2[0:1, 1:2], in_=qsc.ap[0:1, sl2:sl2 + 1]), reads=[qsc])
        p.dma_g(lambda e, sl2=slot: e.dma_start(
            out=qsc.ap[:, sl2:sl2 + 1], in_=d_sc2[0:1, 1:2].partition_broadcast(128)),
            writes=[qsc], extra=[t1])

    def make_scale(raw_slot, scale_slot, inv_slot):
        p.do("vector", lambda e, a=raw_slot, b=scale_slot: e.tensor_scalar(
            out=qsc.ap[:, b:b + 1], in0=qsc.ap[:, a:a + 1], scalar1=1e-6,
            scalar2=1.0 / 127.0, op0=ALU.max, op1=ALU.mult), reads=[qsc], writes=[qsc])
        p.do("vector", lambda e, b=scale_slot, c=inv_slot: e.reciprocal(
            out=qsc.ap[:, c:c + 1], in_=qsc.ap[:, b:b + 1]), reads=[qsc], writes=[qsc])

    def dual_round_to(tb, t_ap, outb, out_ap, s1b, s1ap, s2b, s2ap):
        p.do("vector", lambda e, i=t_ap, o=s1ap: e.tensor_scalar(
            out=o, in0=i, scalar1=C23, scalar2=C23, op0=ALU.add, op1=ALU.subtract),
            reads=[tb], writes=[s1b])
        p.do("vector", lambda e, i=t_ap, o=s2ap: e.tensor_scalar(
            out=o, in0=i, scalar1=C52, scalar2=C52, op0=ALU.add, op1=ALU.subtract),
            reads=[tb], writes=[s2b])
        p.do("vector", lambda e, i0=s1ap, i1=s2ap, o=out_ap: e.scalar_tensor_tensor(
            out=o, in0=i0, scalar=delta.ap, in1=i1, op0=ALU.mult, op1=ALU.add),
            reads=[s1b, s2b, delta], writes=[outb])

    def max_accum(src_ap, src_bufs, slot, first):
        p.do("vector", lambda e, i=src_ap: e.tensor_reduce(
            out=stat.ap[:, 0:1], in_=i, axis=AX, op=ALU.max,
            apply_absolute_value=True), reads=src_bufs, writes=[stat])
        if first:
            p.do("vector", lambda e, sl2=slot: e.tensor_copy(
                out=qsc.ap[:, sl2:sl2 + 1], in_=stat.ap[:, 0:1]),
                reads=[stat], writes=[qsc])
        else:
            p.do("vector", lambda e, sl2=slot: e.tensor_tensor(
                out=qsc.ap[:, sl2:sl2 + 1], in0=qsc.ap[:, sl2:sl2 + 1],
                in1=stat.ap[:, 0:1], op=ALU.max), reads=[stat], writes=[qsc])

    def l0_ffn():
        p.barrier()
        c2a = _Carve(AR16, 24704)
        XQ = c2a.take(1024, name="XQ")
        W1Q = [c2a.take(4096, name="W1Qa"), c2a.take(4096, name="W1Qb")]
        HHr = [c2a.take(QC, name=f"HH{i}") for i in range(4)]
        HLr = [c2a.take(QC, name=f"HL{i}") for i in range(4)]
        HST = c2a.take(2048, name="HST")
        HLO = c2a.take(2048, name="HLO")
        for qt in range(NQT):
            max_accum(X[qt].ap[:, :], [X[qt]], 0, qt == 0)
        pmax_roundtrip(qsc.ap[:, 0:1], qsc, 0)
        allreduce_max_slot(0)
        make_scale(0, 1, 2)
        for qt in range(NQT):
            for hf in range(2):
                p.do("vector", lambda e, i=X[qt].ap[:, hf * QC:(hf + 1) * QC], o=SC[2].ap:
                     e.tensor_scalar(out=o, in0=i, scalar1=qsc.ap[:, 2:3], scalar2=None,
                                     op0=ALU.mult), reads=[X[qt], qsc], writes=[SC[2]])
                dual_round_to(SC[2], SC[2].ap, XQ, XQ.ap[:, hf * QC:(hf + 1) * QC],
                              SC[0], SC[0].ap, SC[1], SC[1].ap)
            for cc in range(NDC):
                pt = PT[cc % 2]
                p.do("tensor", lambda e, i=XQ.ap[:, cc * 128:(cc + 1) * 128], o=pt.ap[:, :]:
                     e.transpose(o, i, ident.ap[:, :]), reads=[XQ, ident], writes=[pt])
                p.do("vector", lambda e, i=pt.ap[:, :],
                     o=XTH[cc].ap[:, qt * 128:(qt + 1) * 128]:
                     e.tensor_copy(out=o, in_=i), reads=[pt], writes=[XTH[cc]])
        for c_ in range(NDC):
            for hf in range(2):
                p.dma_s(lambda e, o=WS32.ap[:, :],
                     i=d_w1t[c_][:, hf * 2048:(hf + 1) * 2048]:
                     e.dma_start(out=o, in_=i), writes=[WS32])
                max_accum(WS32.ap[:, :], [WS32], 3, c_ == 0 and hf == 0)
        for c2 in range(32):
            p.dma_s(lambda e, o=WS32.ap[:, 0:1024], i=d_w2t[c2]:
                 e.dma_start(out=o, in_=i), writes=[WS32])
            max_accum(WS32.ap[:, 0:1024], [WS32], 6, c2 == 0)
        pmax_roundtrip(qsc.ap[:, 3:4], qsc, 3)
        bcast_slot(3)
        make_scale(3, 4, 5)
        pmax_roundtrip(qsc.ap[:, 6:7], qsc, 6)
        bcast_slot(6)
        make_scale(6, 7, 8)
        p.do("vector", lambda e: e.tensor_tensor(
            out=qsc.ap[:, 9:10], in0=qsc.ap[:, 1:2], in1=qsc.ap[:, 4:5], op=ALU.mult),
            reads=[qsc], writes=[qsc])
        for c_ in range(NDC):
            wq16 = W1Q[c_ % 2]
            for hf in range(2):
                p.dma_s(lambda e, o=WS32.ap[:, :],
                     i=d_w1t[c_][:, hf * 2048:(hf + 1) * 2048]:
                     e.dma_start(out=o, in_=i), writes=[WS32])
                for pc in range(2):
                    p.do("vector", lambda e, i=WS32.ap[:, pc * 1024:(pc + 1) * 1024],
                         o=h32.ap[:, 0:1024]: e.tensor_scalar(
                        out=o, in0=i, scalar1=qsc.ap[:, 5:6], scalar2=None,
                        op0=ALU.mult), reads=[WS32, qsc], writes=[h32])
                    dual_round_to(h32, h32.ap[:, 0:1024], wq16,
                                  wq16.ap[:, hf * 2048 + pc * 1024:
                                          hf * 2048 + (pc + 1) * 1024],
                                  h32, h32.ap[:, 1024:2048], brow, brow.ap)
            p.dma_g(lambda e, i=wq16.ap, o=d_w1q[c_]: e.dma_start(out=o, in_=i),
                    reads=[wq16], writes=[b_w1q])
        for c2 in range(32):
            wq16 = W1Q[c2 % 2]
            p.dma_s(lambda e, o=WS32.ap[:, 0:1024], i=d_w2t[c2]:
                 e.dma_start(out=o, in_=i), writes=[WS32])
            p.do("vector", lambda e, i=WS32.ap[:, 0:1024], o=h32.ap[:, 0:1024]:
                 e.tensor_scalar(out=o, in0=i, scalar1=qsc.ap[:, 8:9], scalar2=None,
                                 op0=ALU.mult), reads=[WS32, qsc], writes=[h32])
            dual_round_to(h32, h32.ap[:, 0:1024], wq16, wq16.ap[:, 0:1024],
                          h32, h32.ap[:, 1024:2048], brow, brow.ap)
            p.dma_g(lambda e, i=wq16.ap[:, 0:1024], o=d_w2q[c2]:
                    e.dma_start(out=o, in_=i), reads=[wq16], writes=[b_w2q])
        p.flush_g([b_w1q, b_w2q])
        p.dma_s(lambda e: e.dma_start(out=gates.ap[:, 0:32], in_=d_b1),
             writes=[gates])
        p.do("vector", lambda e: e.tensor_scalar(
            out=gates.ap[:, 0:32], in0=gates.ap[:, 0:32], scalar1=qsc.ap[:, 9:10],
            scalar2=None, op0=ALU.mult), reads=[gates, qsc], writes=[gates])
        for grp in range(8):
            wq16 = W1Q[grp % 2]
            p.dma_s(lambda e, o=wq16.ap,
                 i=d_w1q[:, :, grp * 512:(grp + 1) * 512].transpose([1, 0, 2]):
                 e.dma_start(out=o, in_=i), reads=[b_w1q], writes=[wq16])
            for od4 in range(4):
                odg = grp * 4 + od4
                for tc in range(NQC):
                    ps = PS[(od4 * NQC + tc) % 2]
                    for c_ in range(NDC):
                        p.do("tensor", lambda e, o=ps.ap[:, :],
                             a=wq16.ap[:, c_ * 512 + od4 * 128:c_ * 512 + (od4 + 1) * 128],
                             b=XTH[c_].ap[:, tc * QC:(tc + 1) * QC], s0=(c_ == 0),
                             s1=(c_ == NDC - 1):
                             e.matmul(o, a, b, start=s0, stop=s1),
                             reads=[wq16, XTH[c_]], writes=[ps])
                    p.do("scalar", lambda e, i=ps.ap[:, :], o=SC[2].ap,
                         b=gates.ap[:, odg:odg + 1]: e.activation(
                        out=o, in_=i, func=AF.Gelu, bias=b, scale=qsc.ap[:, 9:10]),
                        reads=[ps, gates, qsc], writes=[SC[2]])
                    max_accum(SC[2].ap, [SC[2]], 10, grp == 0 and od4 == 0 and tc == 0)
                    HH = HHr[(od4 * NQC + tc) % 4]
                    HL = HLr[(od4 * NQC + tc) % 4]
                    split_hi_lo(SC[2], SC[2].ap, HH, HH.ap, HL, HL.ap)
                    p.do("gpsimd", lambda e, i=HH.ap,
                         o=d_hhi[odg * 128:(odg + 1) * 128, tc * QC:(tc + 1) * QC]:
                         e.dma_start(out=o, in_=i), reads=[HH], writes=[b_hhi], dma=True)
                    p.do("gpsimd", lambda e, i=HL.ap,
                         o=d_hlo[odg * 128:(odg + 1) * 128, tc * QC:(tc + 1) * QC]:
                         e.dma_start(out=o, in_=i), reads=[HL], writes=[b_hlo], dma=True)
        pmax_roundtrip(qsc.ap[:, 10:11], qsc, 10)
        allreduce_max_slot(10)
        make_scale(10, 11, 0)
        p.do("vector", lambda e: e.tensor_tensor(
            out=qsc.ap[:, 2:3], in0=qsc.ap[:, 11:12], in1=qsc.ap[:, 7:8], op=ALU.mult),
            reads=[qsc], writes=[qsc])
        for c2 in range(32):
            p.dma_g(lambda e, o=h32.ap, i=d_hhi[c2 * 128:(c2 + 1) * 128, :]:
                    e.dma_start(out=o, in_=i), reads=[b_hhi], writes=[h32])
            p.dma_g(lambda e, o=HLO.ap, i=d_hlo[c2 * 128:(c2 + 1) * 128, :]:
                    e.dma_start(out=o, in_=i), reads=[b_hlo], writes=[HLO])
            p.do("vector", lambda e: e.tensor_tensor(
                out=h32.ap, in0=h32.ap, in1=HLO.ap, op=ALU.add),
                reads=[h32, HLO], writes=[h32])
            for pc in range(4):
                p.do("vector", lambda e, i=h32.ap[:, pc * QC:(pc + 1) * QC]:
                     e.tensor_scalar(out=i, in0=i, scalar1=qsc.ap[:, 0:1], scalar2=None,
                                     op0=ALU.mult), reads=[h32, qsc], writes=[h32])
                dual_round_to(h32, h32.ap[:, pc * QC:(pc + 1) * QC], HST,
                              HST.ap[:, pc * QC:(pc + 1) * QC],
                              SC[0], SC[0].ap, SC[1], SC[1].ap)
            p.dma_g(lambda e, i=HST.ap, o=d_hq[c2]: e.dma_start(out=o, in_=i),
                    reads=[HST], writes=[b_hq])
        if debug_stop == "p2sc":
            p.do("vector", lambda e: e.tensor_copy(out=X[0].ap[:, 0:12], in_=qsc.ap[:, :]),
                 reads=[qsc], writes=[X[0]])
            return
        p.flush_g([b_hq])
        p.barrier()
        c2b = _Carve(AR16, 24704)
        W2R = c2b.take(16384, name="W2R")
        HQG = [c2b.take(4096, name="HQGa"), c2b.take(4096, name="HQGb")]
        p.dma_g(lambda e: e.dma_start(out=brow.ap,
                                      in_=d_b2.partition_broadcast(128)),
                writes=[brow])
        p.do("vector", lambda e: e.tensor_scalar(
            out=brow.ap, in0=brow.ap, scalar1=qsc.ap[:, 2:3], scalar2=None,
            op0=ALU.mult), reads=[brow, qsc], writes=[brow])
        for qt in range(NQT):
            p.do("vector", lambda e, i=X[qt].ap[:, :]: e.tensor_tensor(
                out=i, in0=i, in1=brow.ap, op=ALU.add), reads=[X[qt], brow],
                writes=[X[qt]])
        for og2 in range(2):
            for gq in range(4):
                p.dma_s(lambda e,
                     o=W2R.ap[:, gq * 4096:(gq + 1) * 4096],
                     i=d_w2q[gq * 8:(gq + 1) * 8, :, og2 * 512:(og2 + 1) * 512]
                     .transpose([1, 0, 2]):
                     e.dma_start(out=o, in_=i), reads=[b_w2q], writes=[W2R])
            for qt in range(NQT):
                hq16 = HQG[qt % 2]
                p.dma_s(lambda e, o=hq16.ap,
                     i=d_hq[:, :, qt * 128:(qt + 1) * 128].transpose([1, 0, 2]):
                     e.dma_start(out=o, in_=i), reads=[b_hq], writes=[hq16])
                ps = PS[qt % 4]
                for c2 in range(32):
                    p.do("tensor", lambda e, o=ps.ap[:, :],
                         a=hq16.ap[:, c2 * 128:(c2 + 1) * 128],
                         b=W2R.ap[:, c2 * 512:(c2 + 1) * 512], s0=(c2 == 0),
                         s1=(c2 == 31): e.matmul(o, a, b, start=s0, stop=s1),
                         reads=[hq16, W2R], writes=[ps])
                p.do("vector", lambda e, i=ps.ap[:, :],
                     o=X[qt].ap[:, og2 * QC:(og2 + 1) * QC]: e.scalar_tensor_tensor(
                    out=o, in0=i, scalar=qsc.ap[:, 2:3], in1=o, op0=ALU.mult,
                    op1=ALU.add), reads=[ps, X[qt], qsc], writes=[X[qt]])
        layernorm_inplace()

    l0_ffn()
    if debug_stop in ("p2", "p2sc"):
        store_out()
        with es, nc.Block() as block:
            p.emit(block)
        return nc

    def l1_attention():
        p.barrier()
        p.do("vector", lambda e: e.memset(onesrow16b.ap[:, :], 1.0), writes=[onesrow16b])
        build_xt()
        p.dma_g(lambda e: e.dma_start(out=SC[0].ap[:, 0:64],
                                      in_=d_jrow.partition_broadcast(128)),
                writes=[SC[0]])
        p.do("vector", lambda e: e.tensor_scalar(
            out=SC[0].ap[:, 64:128], in0=SC[0].ap[:, 0:64], scalar1=-C_LN_HI,
            scalar2=None, op0=ALU.mult), reads=[SC[0]], writes=[SC[0]])
        p.do("vector", lambda e: e.scalar_tensor_tensor(
            out=SC[0].ap[:, 128:192], in0=SC[0].ap[:, 0:64], scalar=-C_LN_LO,
            in1=SC[0].ap[:, 64:128], op0=ALU.mult, op1=ALU.add),
            reads=[SC[0]], writes=[SC[0]])
        p.do("scalar", lambda e: e.activation(
            out=SC[0].ap[:, 0:64], in_=SC[0].ap[:, 128:192], func=AF.Exp),
            reads=[SC[0]], writes=[SC[0]])
        for qt in range(NQT):
            p.do("gpsimd", lambda e, b=qt * 128: e.iota(
                scal.ap[:, 14:15], pattern=[[0, 1]], base=b, channel_multiplier=1,
                allow_small_or_imprecise_dtypes=True), writes=[scal])
            for trig in range(2):
                p.do("vector", lambda e: e.tensor_scalar(
                    out=SC[1].ap[:, 0:64], in0=SC[0].ap[:, 0:64],
                    scalar1=scal.ap[:, 14:15], scalar2=None, op0=ALU.mult),
                    reads=[SC[0], scal], writes=[SC[1]])
                if trig == 1:
                    p.do("vector", lambda e: e.tensor_scalar(
                        out=SC[1].ap[:, 0:64], in0=SC[1].ap[:, 0:64], scalar1=HALF_PI,
                        scalar2=None, op0=ALU.add), reads=[SC[1]], writes=[SC[1]])
                p.do("vector", lambda e: e.tensor_scalar(
                    out=SC[1].ap[:, 64:128], in0=SC[1].ap[:, 0:64], scalar1=INV_2PI,
                    scalar2=None, op0=ALU.mult), reads=[SC[1]], writes=[SC[1]])
                dual_round_to(SC[1], SC[1].ap[:, 64:128], SC[1], SC[1].ap[:, 64:128],
                              SC[1], SC[1].ap[:, 128:192], SC[1], SC[1].ap[:, 192:256])
                p.do("vector", lambda e: e.scalar_tensor_tensor(
                    out=SC[1].ap[:, 128:192], in0=SC[1].ap[:, 64:128],
                    scalar=-C2PI_HI, in1=SC[1].ap[:, 0:64], op0=ALU.mult, op1=ALU.add),
                    reads=[SC[1]], writes=[SC[1]])
                p.do("vector", lambda e: e.scalar_tensor_tensor(
                    out=SC[1].ap[:, 192:256], in0=SC[1].ap[:, 64:128],
                    scalar=-C2PI_MID, in1=SC[1].ap[:, 128:192], op0=ALU.mult,
                    op1=ALU.add), reads=[SC[1]], writes=[SC[1]])
                dst = SIN[qt] if trig == 0 else COS[qt]
                p.do("scalar", lambda e, o=dst.ap: e.activation(
                    out=o, in_=SC[1].ap[:, 192:256], func=AF.Sin),
                    reads=[SC[1]], writes=[dst])
        if debug_stop == "p3tab":
            for qt in range(NQT):
                p.do("vector", lambda e, o=X[qt].ap[:, 0:64], i=COS[qt].ap:
                     e.tensor_copy(out=o, in_=i), reads=[COS[qt]], writes=[X[qt]])
                p.do("vector", lambda e, o=X[qt].ap[:, 64:128], i=SIN[qt].ap:
                     e.tensor_copy(out=o, in_=i), reads=[SIN[qt]], writes=[X[qt]])
            return
        wslots = [Buf(WS32.ap[:, 0:384], "ws0"), Buf(WS32.ap[:, 384:768], "ws1")]
        woslots = [Buf(WS32.ap[:, 768:1280], "wo0"), Buf(WS32.ap[:, 1280:1792], "wo1")]
        for h in range(8):
            for c_ in range(NDC):
                slb = wslots[c_ % 2]
                p.dma_s(lambda e, o=slb.ap, i=d_wqkv1[h, c_]:
                     e.dma_start(out=o, in_=i), writes=[slb])
                split_hi_lo(slb, slb.ap, WAH, WAH.ap[:, c_ * 384:(c_ + 1) * 384],
                            WAL, WAL.ap[:, c_ * 384:(c_ + 1) * 384])
            for half in range(2):
                slb = woslots[half]
                p.dma_s(lambda e, o=slb.ap,
                     i=d_wo1[h][:, half * 512:(half + 1) * 512]:
                     e.dma_start(out=o, in_=i), writes=[slb])
                split_hi_lo(slb, slb.ap, WBH, WBH.ap[:, half * 512:(half + 1) * 512],
                            WBL, WBL.ap[:, half * 512:(half + 1) * 512])
            for qt in range(NQT):
                ps = PS[qt % 2]
                for c_ in range(NDC):
                    mm3(ps, ps.ap[:, 0:384],
                        XTH[c_], XTH[c_].ap[:, qt * 128:(qt + 1) * 128],
                        XTL[c_], XTL[c_].ap[:, qt * 128:(qt + 1) * 128],
                        WAH, WAH.ap[:, c_ * 384:(c_ + 1) * 384],
                        WAL, WAL.ap[:, c_ * 384:(c_ + 1) * 384],
                        start=(c_ == 0), stop=(c_ == NDC - 1))
                p.do("scalar", lambda e, i=ps.ap[:, 0:384], o=SC[2].ap[:, 0:384]:
                     e.activation(out=o, in_=i, func=AF.Copy),
                     reads=[ps], writes=[SC[2]])
                split_hi_lo(SC[2], SC[2].ap[:, 256:384], VH[qt], VH[qt].ap,
                            VL[qt], VL[qt].ap)
                for which, (dsth, dstl) in enumerate(((QH, QL), (KH, KL))):
                    sl_ap = SC[2].ap[:, which * 128:(which + 1) * 128]
                    p.do("scalar", lambda e, i=sl_ap: e.activation(
                        out=SC[1].ap[:, 0:128], in_=i, func=AF.Square,
                        accum_out=stat.ap[:, 0:1]), reads=[SC[2]], writes=[SC[1], stat])
                    p.do("vector", lambda e: e.tensor_scalar(
                        out=stat.ap[:, 1:2], in0=stat.ap[:, 0:1], scalar1=1.0 / 128.0,
                        scalar2=None, op0=ALU.mult), reads=[stat], writes=[stat])
                    p.do("scalar", lambda e: e.activation(
                        out=stat.ap[:, 2:3], in_=stat.ap[:, 1:2], func=AF.Sqrt,
                        bias=1e-6), reads=[stat, const32], writes=[stat])
                    p.do("vector", lambda e: e.reciprocal(
                        out=stat.ap[:, 3:4], in_=stat.ap[:, 2:3]),
                        reads=[stat], writes=[stat])
                    p.do("vector", lambda e, i=sl_ap: e.tensor_scalar(
                        out=SC[1].ap[:, 0:128], in0=i, scalar1=stat.ap[:, 3:4],
                        scalar2=None, op0=ALU.mult), reads=[SC[2], stat],
                        writes=[SC[1]])
                    p.do("vector", lambda e, c=COS[qt]: e.tensor_tensor(
                        out=SC[1].ap[:, 128:192], in0=SC[1].ap[:, 0:64], in1=c.ap,
                        op=ALU.mult), reads=[SC[1], COS[qt]], writes=[SC[1]])
                    p.do("vector", lambda e, s=SIN[qt]: e.tensor_tensor(
                        out=SC[1].ap[:, 192:256], in0=SC[1].ap[:, 64:128], in1=s.ap,
                        op=ALU.mult), reads=[SC[1], SIN[qt]], writes=[SC[1]])
                    p.do("vector", lambda e: e.tensor_tensor(
                        out=SC[1].ap[:, 128:192], in0=SC[1].ap[:, 128:192],
                        in1=SC[1].ap[:, 192:256], op=ALU.subtract),
                        reads=[SC[1]], writes=[SC[1]])
                    p.do("vector", lambda e, c=COS[qt]: e.tensor_tensor(
                        out=SC[1].ap[:, 192:256], in0=SC[1].ap[:, 64:128], in1=c.ap,
                        op=ALU.mult), reads=[SC[1], COS[qt]], writes=[SC[1]])
                    p.do("vector", lambda e, s=SIN[qt]: e.tensor_tensor(
                        out=SC[1].ap[:, 256:320], in0=SC[1].ap[:, 0:64], in1=s.ap,
                        op=ALU.mult), reads=[SC[1], SIN[qt]], writes=[SC[1]])
                    p.do("vector", lambda e: e.tensor_tensor(
                        out=SC[1].ap[:, 192:256], in0=SC[1].ap[:, 192:256],
                        in1=SC[1].ap[:, 256:320], op=ALU.add),
                        reads=[SC[1]], writes=[SC[1]])
                    split_hi_lo(SC[1], SC[1].ap[:, 128:256], EH[1], EH[1].ap[:, 0:128],
                                EL[1], EL[1].ap[:, 0:128])
                    for half, (src16, dst16) in enumerate(((EH[1], dsth), (EL[1], dstl))):
                        pt = PT[half]
                        p.do("tensor", lambda e, i=src16.ap[:, 0:128], o=pt.ap[:, :]:
                             e.transpose(o, i, ident.ap[:, :]),
                             reads=[src16, ident], writes=[pt])
                        p.do("vector", lambda e, i=pt.ap[:, :],
                             o=dst16.ap[:, qt * 128:(qt + 1) * 128]:
                             e.tensor_copy(out=o, in_=i), reads=[pt], writes=[dst16])
            if debug_stop == "p3qk" and h == 0:
                for half, (s16h, s16l) in enumerate(((QH, QL), (KH, KL))):
                    for pcc in range(2):
                        p.do("vector", lambda e, o=X[half * 2 + pcc].ap[:, :],
                             i0=s16h.ap[:, pcc * 1024:(pcc + 1) * 1024],
                             i1=s16l.ap[:, pcc * 1024:(pcc + 1) * 1024]:
                             e.tensor_tensor(out=o, in0=i0, in1=i1, op=ALU.add),
                             reads=[s16h, s16l], writes=[X[half * 2 + pcc]])
                for kt in range(8):
                    p.do("vector", lambda e, o=X[4].ap[:, kt * 128:kt * 128 + 128],
                         i0=VH[kt].ap, i1=VL[kt].ap: e.tensor_tensor(
                        out=o, in0=i0, in1=i1, op=ALU.add),
                        reads=[VH[kt], VL[kt]], writes=[X[4]])
                return
            attn_inner(causal=True, layer=1, dbg=(debug_stop == "p3a0" and h == 0))
            if debug_stop == "p3a0" and h == 0:
                return
        layernorm_inplace()

    l1_attention()
    if debug_stop in ("p3", "p3tab", "p3qk", "p3a0"):
        store_out()
        with es, nc.Block() as block:
            p.emit(block)
        return nc

    def l1_ffn():
        p.barrier()
        build_xt()
        c4a = _Carve(AR16, 24704)
        GFH = c4a.take(T, name="GFH"); GFL = c4a.take(T, name="GFL")
        WPH = c4a.take(2048, name="WPH"); WPL = c4a.take(2048, name="WPL")
        PJH = [c4a.take(T, name=f"PJH{i}") for i in range(2)]
        PJL = [c4a.take(T, name=f"PJL{i}") for i in range(2)]
        SMH = c4a.take(8, name="SMH"); SML = c4a.take(8, name="SML")
        WDH = c4a.take(128, name="WDH"); WDL = c4a.take(128, name="WDL")
        for c_ in range(NDC):
            p.dma_s(lambda e, o=WS32.ap[:, 0:256], i=d_wpt[c_]:
                 e.dma_start(out=o, in_=i), writes=[WS32])
            split_hi_lo(WS32, WS32.ap[:, 0:256], WPH, WPH.ap[:, c_ * 256:(c_ + 1) * 256],
                        WPL, WPL.ap[:, c_ * 256:(c_ + 1) * 256])
        for c_ in range(NDC):
            p.dma_s(lambda e, o=WS32.ap[:, 0:15], i=d_wdt[c_]:
                 e.dma_start(out=o, in_=i), writes=[WS32])
            split_hi_lo(WS32, WS32.ap[:, 0:15], WDH, WDH.ap[:, c_ * 16:c_ * 16 + 15],
                        WDL, WDL.ap[:, c_ * 16:c_ * 16 + 15])
        p.dma_s(lambda e: e.dma_start(out=scal.ap[:, 8:10], in_=d_bp),
             writes=[bcol])
        p.dma_s(lambda e: e.dma_start(out=SC[2].ap[0:4, 0:256], in_=d_simt),
             writes=[SC[2]])
        p.do("scalar", lambda e: e.activation(
            out=SC[1].ap[0:4, 0:256], in_=SC[2].ap[0:4, 0:256], func=AF.Square,
            accum_out=stat.ap[0:4, 0:1]), reads=[SC[2]], writes=[SC[1], stat])
        p.do("scalar", lambda e: e.activation(
            out=stat.ap[0:4, 1:2], in_=stat.ap[0:4, 0:1], func=AF.Sqrt),
            reads=[stat], writes=[stat])
        p.do("vector", lambda e: e.tensor_scalar(
            out=stat.ap[0:4, 1:2], in0=stat.ap[0:4, 1:2], scalar1=1e-12, scalar2=None,
            op0=ALU.max), reads=[stat], writes=[stat])
        p.do("vector", lambda e: e.reciprocal(out=stat.ap[0:4, 2:3],
                                              in_=stat.ap[0:4, 1:2]),
             reads=[stat], writes=[stat])
        p.dma_g(lambda e: e.dma_start(out=stat.ap[0:1, 3:4], in_=d_temp),
                writes=[stat])
        p.do("vector", lambda e: e.tensor_scalar(
            out=stat.ap[0:1, 3:4], in0=stat.ap[0:1, 3:4], scalar1=CLAMP_MAX,
            scalar2=None, op0=ALU.min), reads=[stat], writes=[stat])
        p.do("scalar", lambda e: e.activation(
            out=stat.ap[0:1, 4:5], in_=stat.ap[0:1, 3:4], func=AF.Exp),
            reads=[stat], writes=[stat])
        t1 = p.dma_g(lambda e: e.dma_start(out=d_sc2[0:1, 2:3],
                                           in_=stat.ap[0:1, 4:5]), reads=[stat])
        p.dma_g(lambda e: e.dma_start(
            out=stat.ap[:, 4:5], in_=d_sc2[0:1, 2:3].partition_broadcast(128)),
            writes=[stat], extra=[t1])
        p.do("vector", lambda e: e.tensor_scalar(
            out=SC[2].ap[0:4, 0:256], in0=SC[2].ap[0:4, 0:256],
            scalar1=stat.ap[0:4, 2:3], scalar2=stat.ap[0:4, 4:5], op0=ALU.mult,
            op1=ALU.mult), reads=[SC[2], stat], writes=[SC[2]])
        split_hi_lo(SC[2], SC[2].ap[0:4, 0:256], EH[0], EH[0].ap[0:4, 0:256],
                    EL[0], EL[0].ap[0:4, 0:256])
        for cc in range(2):
            for half, (src16, dstb) in enumerate(((EH[0], SMH), (EL[0], SML))):
                pt = PT[half]
                p.do("tensor", lambda e, i=src16.ap[0:4, cc * 128:(cc + 1) * 128],
                     o=pt.ap[:, 0:4]: e.transpose(o, i, ident.ap[0:4, 0:4]),
                     reads=[src16, ident], writes=[pt])
                p.do("vector", lambda e, i=pt.ap[:, 0:4], o=dstb.ap[:, cc * 4:cc * 4 + 4]:
                     e.tensor_copy(out=o, in_=i), reads=[pt], writes=[dstb])
        for tc in range(NQC):
            for fc in range(2):
                ps = PS[fc]
                for c_ in range(NDC):
                    mm3(ps, ps.ap[:, :],
                        WPH, WPH.ap[:, c_ * 256 + fc * 128:c_ * 256 + (fc + 1) * 128],
                        WPL, WPL.ap[:, c_ * 256 + fc * 128:c_ * 256 + (fc + 1) * 128],
                        XTH[c_], XTH[c_].ap[:, tc * QC:(tc + 1) * QC],
                        XTL[c_], XTL[c_].ap[:, tc * QC:(tc + 1) * QC],
                        start=(c_ == 0), stop=(c_ == NDC - 1))
                dst32 = SC[0] if fc == 0 else SC[2]
                p.do("scalar", lambda e, i=ps.ap[:, :], o=dst32.ap,
                     b=scal.ap[:, 8 + fc:9 + fc]: e.activation(
                    out=o, in_=i, func=AF.Identity, bias=b),
                    reads=[ps, bcol], writes=[dst32])
                p.do("scalar", lambda e, i=dst32.ap: e.activation(
                    out=SC[1].ap, in_=i, func=AF.Square), reads=[dst32], writes=[SC[1]])
                split_hi_lo(SC[1], SC[1].ap, EH[0], EH[0].ap, EL[0], EL[0].ap)
                p.do("tensor", lambda e, o=PSD.ap[0:1, :], a=onescol16.ap[:, 0:1],
                     b=EH[0].ap, s=(fc == 0): e.matmul(o, a, b, start=s, stop=False,
                                                       skip_group_check=True),
                     reads=[onescol16, EH[0]], writes=[PSD])
                p.do("tensor", lambda e, o=PSD.ap[0:1, :], a=onescol16.ap[:, 0:1],
                     b=EL[0].ap, s=(fc == 1): e.matmul(o, a, b, start=False, stop=s,
                                                       skip_group_check=True),
                     reads=[onescol16, EL[0]], writes=[PSD])
            p.do("scalar", lambda e: e.activation(
                out=drow_b.ap[0:1, :], in_=PSD.ap[0:1, :], func=AF.Sqrt),
                reads=[PSD], writes=[drow_b])
            p.do("vector", lambda e: e.tensor_scalar(
                out=drow_b.ap[0:1, :], in0=drow_b.ap[0:1, :], scalar1=1e-12,
                scalar2=None, op0=ALU.max), reads=[drow_b], writes=[drow_b])
            p.do("vector", lambda e: e.reciprocal(out=drow_b.ap[0:1, :],
                                                  in_=drow_b.ap[0:1, :]),
                 reads=[drow_b], writes=[drow_b])
            split_hi_lo(drow_b, drow_b.ap[0:1, :], drh_b, drh_b.ap[0:1, :],
                        drl_b, drl_b.ap[0:1, :])
            p.do("tensor", lambda e, o=PS[3].ap[:, :], a=onesrow16b.ap[0:1, 0:128],
                 b=drh_b.ap[0:1, :]: e.matmul(o, a, b, start=True, stop=False,
                                              skip_group_check=True),
                 reads=[onesrow16b, drh_b], writes=[PS[3]])
            p.do("tensor", lambda e, o=PS[3].ap[:, :], a=onesrow16b.ap[0:1, 0:128],
                 b=drl_b.ap[0:1, :]: e.matmul(o, a, b, start=False, stop=True,
                                              skip_group_check=True),
                 reads=[onesrow16b, drl_b], writes=[PS[3]])
            p.do("scalar", lambda e: e.activation(out=bc32.ap, in_=PS[3].ap[:, :],
                                                  func=AF.Copy),
                 reads=[PS[3]], writes=[bc32])
            for fc in range(2):
                src32 = SC[0] if fc == 0 else SC[2]
                p.do("vector", lambda e, i=src32.ap: e.tensor_tensor(
                    out=i, in0=i, in1=bc32.ap, op=ALU.mult),
                    reads=[src32, bc32], writes=[src32])
                split_hi_lo(src32, src32.ap, PJH[fc],
                            PJH[fc].ap[:, tc * QC:(tc + 1) * QC],
                            PJL[fc], PJL[fc].ap[:, tc * QC:(tc + 1) * QC])
        for qt in range(NQT):
            ps = PS[2]
            for fc in range(2):
                mm3(ps, ps.ap[:, 0:4],
                    PJH[fc], PJH[fc].ap[:, qt * 128:(qt + 1) * 128],
                    PJL[fc], PJL[fc].ap[:, qt * 128:(qt + 1) * 128],
                    SMH, SMH.ap[:, fc * 4:(fc + 1) * 4],
                    SML, SML.ap[:, fc * 4:(fc + 1) * 4],
                    start=(fc == 0), stop=(fc == 1))
            p.do("scalar", lambda e, i=ps.ap[:, 0:4]: e.activation(
                out=SC[0].ap[:, 0:4], in_=i, func=AF.Copy), reads=[ps], writes=[SC[0]])
            p.do("vector", lambda e: e.tensor_reduce(
                out=stat.ap[:, 0:1], in_=SC[0].ap[:, 0:4], axis=AX, op=ALU.max),
                reads=[SC[0]], writes=[stat])
            p.do("vector", lambda e: e.tensor_scalar(
                out=stat.ap[:, 1:2], in0=stat.ap[:, 0:1], scalar1=-1.0, scalar2=None,
                op0=ALU.mult), reads=[stat], writes=[stat])
            p.do("scalar", lambda e: e.activation(
                out=SC[0].ap[:, 4:8], in_=SC[0].ap[:, 0:4], func=AF.Exp,
                bias=stat.ap[:, 1:2]), reads=[SC[0], stat], writes=[SC[0]])
            p.do("vector", lambda e: e.tensor_reduce(
                out=stat.ap[:, 2:3], in_=SC[0].ap[:, 4:8], axis=AX, op=ALU.add),
                reads=[SC[0]], writes=[stat])
            p.do("vector", lambda e: e.reciprocal(out=stat.ap[:, 3:4],
                                                  in_=stat.ap[:, 2:3]),
                 reads=[stat], writes=[stat])
            p.do("vector", lambda e: e.tensor_scalar(
                out=SC[0].ap[:, 8:12], in0=SC[0].ap[:, 4:8], scalar1=stat.ap[:, 3:4],
                scalar2=None, op0=ALU.mult), reads=[SC[0], stat], writes=[SC[0]])
            p.do("vector", lambda e: e.tensor_reduce(
                out=stat.ap[:, 4:5], in_=SC[0].ap[:, 8:12], axis=AX, op=ALU.max),
                reads=[SC[0]], writes=[stat])
            for c_ in range(4):
                p.do("vector", lambda e, c2=c_: e.tensor_scalar(
                    out=SC[0].ap[:, 12 + c2:13 + c2], in0=SC[0].ap[:, 8 + c2:9 + c2],
                    scalar1=stat.ap[:, 4:5], scalar2=None, op0=ALU.is_ge),
                    reads=[SC[0], stat], writes=[SC[0]])
            p.do("vector", lambda e: e.tensor_copy(
                out=SC[0].ap[:, 16:17], in_=SC[0].ap[:, 12:13]),
                reads=[SC[0]], writes=[SC[0]])
            p.do("vector", lambda e: e.tensor_scalar(
                out=SC[0].ap[:, 20:21], in0=SC[0].ap[:, 12:13], scalar1=-1.0,
                scalar2=1.0, op0=ALU.mult, op1=ALU.add), reads=[SC[0]], writes=[SC[0]])
            for c_ in range(1, 4):
                p.do("vector", lambda e, c2=c_: e.tensor_tensor(
                    out=SC[0].ap[:, 16 + c2:17 + c2], in0=SC[0].ap[:, 12 + c2:13 + c2],
                    in1=SC[0].ap[:, 20:21], op=ALU.mult), reads=[SC[0]], writes=[SC[0]])
                if c_ < 3:
                    p.do("vector", lambda e, c2=c_: e.tensor_scalar(
                        out=SC[0].ap[:, 21:22], in0=SC[0].ap[:, 12 + c2:13 + c2],
                        scalar1=-1.0, scalar2=1.0, op0=ALU.mult, op1=ALU.add),
                        reads=[SC[0]], writes=[SC[0]])
                    p.do("vector", lambda e: e.tensor_tensor(
                        out=SC[0].ap[:, 20:21], in0=SC[0].ap[:, 20:21],
                        in1=SC[0].ap[:, 21:22], op=ALU.mult),
                        reads=[SC[0]], writes=[SC[0]])
            for c_ in range(4):
                p.do("vector", lambda e, c2=c_, g2=qt * 4 + c_: e.tensor_tensor(
                    out=gates.ap[:, g2:g2 + 1], in0=SC[0].ap[:, 8 + c2:9 + c2],
                    in1=SC[0].ap[:, 16 + c2:17 + c2], op=ALU.mult),
                    reads=[SC[0]], writes=[gates])
            ps2 = PS[3]
            for c_ in range(NDC):
                mm3(ps2, ps2.ap[:, 0:15],
                    XTH[c_], XTH[c_].ap[:, qt * 128:(qt + 1) * 128],
                    XTL[c_], XTL[c_].ap[:, qt * 128:(qt + 1) * 128],
                    WDH, WDH.ap[:, c_ * 16:c_ * 16 + 15],
                    WDL, WDL.ap[:, c_ * 16:c_ * 16 + 15],
                    start=(c_ == 0), stop=(c_ == NDC - 1))
            p.do("scalar", lambda e, i=ps2.ap[:, 0:15]: e.activation(
                out=SC[1].ap[:, 0:15], in_=i, func=AF.Copy), reads=[ps2], writes=[SC[1]])
            off15 = 0
            for e_, r_ in enumerate((1, 2, 4, 8)):
                p.do("vector", lambda e, o15=off15, r2=r_, g2=qt * 4 + e_:
                     e.tensor_scalar(out=SC[1].ap[:, 16 + o15:16 + o15 + r2],
                                     in0=ones32.ap[:, 0:r2],
                                     scalar1=gates.ap[:, g2:g2 + 1], scalar2=None,
                                     op0=ALU.mult), reads=[ones32, gates],
                     writes=[SC[1]])
                off15 += r_
            p.do("vector", lambda e: e.tensor_tensor(
                out=SC[1].ap[:, 32:47], in0=SC[1].ap[:, 0:15], in1=SC[1].ap[:, 16:31],
                op=ALU.mult), reads=[SC[1]], writes=[SC[1]])
            split_hi_lo(SC[1], SC[1].ap[:, 32:47], EH[1], EH[1].ap[:, 0:15],
                        EL[1], EL[1].ap[:, 0:15])
            for half, (src16, dst16) in enumerate(((EH[1], GFH), (EL[1], GFL))):
                pt = PT[half]
                p.do("tensor", lambda e, i=src16.ap[:, 0:15], o=pt.ap[0:15, :]:
                     e.transpose(o, i, ident.ap[:, :]), reads=[src16, ident],
                     writes=[pt])
                p.do("vector", lambda e, i=pt.ap[0:15, :],
                     o=dst16.ap[0:15, qt * 128:(qt + 1) * 128]:
                     e.tensor_copy(out=o, in_=i), reads=[pt], writes=[dst16])
        p.barrier()
        c4b = _Carve(AR16, 24704)
        c4b.off = 2 * T
        WTH = c4b.take(4096, name="WTH"); WTL = c4b.take(4096, name="WTL")
        WUH = c4b.take(1024, rows=15, name="WUH"); WUL = c4b.take(1024, rows=15, name="WUL")
        p.dma_s(lambda e: e.dma_start(out=WS32.ap[0:15, 0:1024], in_=d_wut),
             writes=[WS32])
        split_hi_lo(WS32, WS32.ap[0:15, 0:1024], WUH, WUH.ap, WUL, WUL.ap)
        wslots = [Buf(WS32.ap[:, 0:512], "w4a"), Buf(WS32.ap[:, 512:1024], "w4b")]
        for og in range(2):
            for c_ in range(NDC):
                slb = wslots[c_ % 2]
                p.dma_s(lambda e, o=slb.ap, i=d_wt1[c_][:, og * 512:(og + 1) * 512]:
                     e.dma_start(out=o, in_=i), writes=[slb])
                split_hi_lo(slb, slb.ap, WTH, WTH.ap[:, c_ * 512:(c_ + 1) * 512],
                            WTL, WTL.ap[:, c_ * 512:(c_ + 1) * 512])
            for qt in range(NQT):
                ps = PS[qt % 2]
                for c_ in range(NDC):
                    mm3(ps, ps.ap[:, :],
                        XTH[c_], XTH[c_].ap[:, qt * 128:(qt + 1) * 128],
                        XTL[c_], XTL[c_].ap[:, qt * 128:(qt + 1) * 128],
                        WTH, WTH.ap[:, c_ * 512:(c_ + 1) * 512],
                        WTL, WTL.ap[:, c_ * 512:(c_ + 1) * 512],
                        start=(c_ == 0), stop=False)
                mm3(ps, ps.ap[:, :],
                    GFH, GFH.ap[0:15, qt * 128:(qt + 1) * 128],
                    GFL, GFL.ap[0:15, qt * 128:(qt + 1) * 128],
                    WUH, WUH.ap[0:15, og * 512:(og + 1) * 512],
                    WUL, WUL.ap[0:15, og * 512:(og + 1) * 512],
                    start=False, stop=True)
                p.do("vector", lambda e, i=ps.ap[:, :],
                     o=X[qt].ap[:, og * QC:(og + 1) * QC]: e.scalar_tensor_tensor(
                    out=o, in0=i, scalar=1.0, in1=o, op0=ALU.mult, op1=ALU.add),
                    reads=[ps, X[qt]], writes=[X[qt]])
        layernorm_inplace()

    l1_ffn()
    if debug_stop == "p4":
        store_out()
        with es, nc.Block() as block:
            p.emit(block)
        return nc
    store_out()
    with es, nc.Block() as block:
        p.emit(block)
    return nc

def _host_inputs(inputs):
    f32 = np.float32
    g = {k: np.asarray(v) for k, v in inputs.items()}
    out = {}
    out["ident"] = np.eye(128, dtype=ml_dtypes.bfloat16)
    k_idx = np.arange(128)[:, None]
    q_idx = np.arange(128)[None, :]
    out["cmask"] = np.where(k_idx > q_idx, f32(NEG_BIG), f32(0.0)).astype(f32)
    out["pos"] = np.arange(T, dtype=f32)[:, None].copy()
    out["jrow"] = np.arange(64, dtype=f32)[None, :].copy()
    wq = np.ascontiguousarray(g["L0_attn_W_Q"].reshape(1024, 1024).T)
    wk = np.ascontiguousarray(g["L0_attn_W_K"].reshape(1024, 1024).T)
    wv = np.ascontiguousarray(g["L0_attn_W_V"].reshape(1024, 1024).T)
    wqkv = np.empty((8, 8, 128, 384), dtype=f32)
    for hp in range(8):
        s = slice(hp * 128, (hp + 1) * 128)
        wqkv[hp] = np.concatenate([wq[:, s], wk[:, s], wv[:, s]], axis=1).reshape(8, 128, 384)
    out["l0_wqkv"] = wqkv
    bq = g["L0_attn_b_Q"].reshape(1024)
    bk = g["L0_attn_b_K"].reshape(1024)
    bv = g["L0_attn_b_V"].reshape(1024)
    bqkv = np.empty((8, 128, 3), dtype=f32)
    for hp in range(8):
        s = slice(hp * 128, (hp + 1) * 128)
        bqkv[hp] = np.stack([bq[s], bk[s], bv[s]], axis=1)
    out["l0_bqkv"] = bqkv
    wo = np.ascontiguousarray(g["L0_attn_W_O"].transpose(0, 2, 1).reshape(1024, 1024))
    out["l0_wo"] = np.ascontiguousarray(wo.reshape(8, 128, 1024))
    out["l0_bo"] = np.ascontiguousarray(g["L0_attn_b_O"].reshape(1, 1024))
    out["l0_w1t"] = np.ascontiguousarray(g["L0_ffn_w1"].T).reshape(8, 128, 4096)
    out["l0_b1"] = np.ascontiguousarray(g["L0_ffn_b1"].reshape(32, 128).T)
    out["l0_w2t"] = np.ascontiguousarray(g["L0_ffn_w2"].T).reshape(32, 128, 1024)
    out["l0_b2"] = np.ascontiguousarray(g["L0_ffn_b2"].reshape(1, 1024))
    w1q = g["L1_attn_wq"].reshape(1024, 1024)
    w1k = g["L1_attn_wk"].reshape(1024, 1024)
    w1v = g["L1_attn_wv"].reshape(1024, 1024)
    wqkv1 = np.empty((8, 8, 128, 384), dtype=f32)
    for h in range(8):
        s = slice(h * 128, (h + 1) * 128)
        wqkv1[h] = np.concatenate([w1q[:, s], w1k[:, s], w1v[:, s]],
                                  axis=1).reshape(8, 128, 384)
    out["l1_wqkv"] = wqkv1
    out["l1_wo"] = np.ascontiguousarray(np.asarray(g["L1_attn_wo"], dtype=f32))
    out["l1_wt"] = np.ascontiguousarray(g["L1_ffn_W"].T).reshape(8, 128, 1024)
    out["l1_wpt"] = np.ascontiguousarray(g["L1_ffn_Wp"].T).reshape(8, 128, 256)
    out["l1_bp"] = np.ascontiguousarray(g["L1_ffn_bp"].reshape(2, 128).T)
    out["l1_simt"] = np.ascontiguousarray(g["L1_ffn_sim"].T)
    out["l1_temp"] = np.ascontiguousarray(g["L1_ffn_temp"].reshape(1, 1))
    wd = np.concatenate([g[f"L1_ffn_Wd{i}"] for i in range(4)], axis=0)
    out["l1_wdt"] = np.ascontiguousarray(wd.T).reshape(8, 128, 15)
    wu = np.concatenate([g[f"L1_ffn_Wu{i}"] for i in range(4)], axis=1)
    out["l1_wut"] = np.ascontiguousarray(wu.T)
    return g, out


def kernel(**inputs):
    g, shared = _host_inputs(inputs)
    x = np.ascontiguousarray(np.asarray(g["L0_attn_x"], dtype=np.float32))
    nc = build()
    in_maps = []
    for b in range(8):
        m = {"x": x[b]}
        m.update(shared)
        in_maps.append(m)
    res = run_bass_kernel_spmd(nc, in_maps, core_ids=list(range(8)))
    return np.stack([res.results[b]["y"] for b in range(8)], axis=0)
